# version 11
# baseline (speedup 1.0000x reference)
import contextlib
import numpy as np
import ml_dtypes
import concourse.bass as bass
import concourse.mybir as mybir
from concourse.bass_utils import run_bass_kernel_spmd

F32 = mybir.dt.float32
BF16 = mybir.dt.bfloat16
AF = mybir.ActivationFunctionType
ALU = mybir.AluOpType

MAGIC = float(np.float32(1.5 * 2 ** 23))
CM1 = float(np.float32(MAGIC - 1.0))
CP1 = float(np.float32(MAGIC + 1.0))
LN_EPS = 1e-5
G_EPS = 1e-8
NW = 5632 * 2048
D = 2048
H = 5632
NCD = 16
NCH = 44
TOK = 2048
HT = 1024
ENGS = ("sync", "gpsimd", "vector", "scalar", "tensor")

USE_BCAST_AP = True


class Sched:

    def __init__(self):
        self.ops = {e: [] for e in ENGS}
        self.cnt = {n: 0 for n in ("qs", "qa", "qg", "qx", "qg1", "qg2", "qg3",
                                   "qL1", "qL2", "qL3", "qc1", "qc2", "qc3",
                                   "qR1", "qR2", "qCC", "qST",
                                   "ep", "ev", "es", "eg")}
        self.hw = {e: {} for e in ENGS}

    def emit(self, eng, fn, waits=(), inc=None, amt=1):
        w = []
        for s, v in waits:
            if v is None or v <= 0:
                continue
            assert v <= self.cnt[s], f"wait {s}>={v} but only {self.cnt[s]} scheduled"
            if self.hw[eng].get(s, 0) >= v:
                continue
            self.hw[eng][s] = v
            w.append((s, v))
        val = None
        if inc is not None:
            self.cnt[inc] += amt
            val = self.cnt[inc]
        self.ops[eng].append((tuple(w), fn, inc, amt))
        return val

    def dma(self, eng, sem, fn, waits=()):
        return self.emit(eng, fn, waits, inc=sem, amt=16)

    def barrier(self):
        snap = [(s, v) for s, v in self.cnt.items()]
        for eng, sem in (("tensor", "ep"), ("vector", "ev"),
                         ("scalar", "es"), ("gpsimd", "eg")):
            self.emit(eng, lambda e: e.nop(), waits=snap, inc=sem)
        snap2 = [(s, v) for s, v in self.cnt.items()]
        for eng in ENGS:
            self.emit(eng, lambda e: e.nop(), waits=snap2)


def build_program(phases=3, dump_h1=False, dump_hid=False, dump_q=False):
    nc = bass.Bass("TRN2", target_bir_lowering=False)

    x_fm = nc.dram_tensor("x_fm", [D, TOK], F32, kind="ExternalInput").ap()
    w1t = nc.dram_tensor("w1t", [NCH, 128, NCD, 128], F32, kind="ExternalInput").ap()
    w2t = nc.dram_tensor("w2t", [NCH, 128, NCD, 128], F32, kind="ExternalInput").ap()
    w3t = nc.dram_tensor("w3t", [NCH, 128, NCD, 128], F32, kind="ExternalInput").ap()
    c_ddiag = nc.dram_tensor("c_ddiag", [128, 128], F32, kind="ExternalInput").ap()
    c_doff = nc.dram_tensor("c_doff", [128, 128], F32, kind="ExternalInput").ap()
    c_tri = nc.dram_tensor("c_tri", [128, 128], BF16, kind="ExternalInput").ap()
    c_offs = nc.dram_tensor("c_offs", [128, 16], F32, kind="ExternalInput").ap()
    c_ones_r_bf = nc.dram_tensor("c_ones_r_bf", [1, 128], BF16, kind="ExternalInput").ap()
    c_ones_c_bf = nc.dram_tensor("c_ones_c_bf", [128, 1], BF16, kind="ExternalInput").ap()
    c_ones_r_f = nc.dram_tensor("c_ones_r_f", [1, 128], F32, kind="ExternalInput").ap()
    c_ones_c_f = nc.dram_tensor("c_ones_c_f", [128, 1], F32, kind="ExternalInput").ap()
    w0c_d = nc.dram_tensor("w0c", [128, 16], F32, kind="ExternalInput").ap()
    biasc_d = nc.dram_tensor("biasc", [128, 16], F32, kind="ExternalInput").ap()
    dv_d = nc.dram_tensor("dv", [1, 2], F32, kind="ExternalInput").ap()
    w1row_d = nc.dram_tensor("w1row", [1, D], F32, kind="ExternalInput").ap()

    out_fm = nc.dram_tensor("out_fm", [D, TOK], F32, kind="ExternalOutput").ap()

    q1c = nc.dram_tensor("q1c", [NCH, 128, D], BF16, kind="Internal").ap()
    q2c = nc.dram_tensor("q2c", [NCH, 128, D], BF16, kind="Internal").ap()
    q3c = nc.dram_tensor("q3c", [NCH, 128, NCD, 128], BF16, kind="Internal").ap()

    dbg_h1 = dbg_hid = dbg_q1 = dbg_q2 = dbg_rg = dbg_ga = None
    if dump_h1:
        dbg_h1 = nc.dram_tensor("dbg_h1", [2, 128, 16 * HT], BF16, kind="ExternalOutput").ap()
    if dump_hid:
        dbg_hid = nc.dram_tensor("dbg_hid", [2, 128, NCH * HT], BF16, kind="ExternalOutput").ap()
    if dump_q:
        dbg_q1 = nc.dram_tensor("dbg_q1", [NCH, 128, D], BF16, kind="ExternalOutput").ap()
        dbg_q2 = nc.dram_tensor("dbg_q2", [NCH, 128, D], BF16, kind="ExternalOutput").ap()
        dbg_rg = nc.dram_tensor("dbg_rg", [128, 3], F32, kind="ExternalOutput").ap()
        dbg_ga = nc.dram_tensor("dbg_ga", [128, 4], F32, kind="ExternalOutput").ap()

    REG = 32768 + 32768 + 90112 + 24576 + 20480
    base, _ = nc.bump_sbuf(REG)
    off_x, off_h1, off_hid, off_qr, off_scr = (
        base, base + 32768, base + 65536, base + 155648, base + 180224)

    def at(name, shape, dtype, off):
        return nc.alloc_sbuf_tensor_at(name, shape, dtype, offset=off)

    XB = at("XB", [128, 16 * HT], BF16, off_x)
    L1 = at("L1", [128, 2048], F32, off_x)
    L2 = at("L2", [128, 2048], F32, off_x + 8192)
    L3 = at("L3", [128, 2048], F32, off_x + 16384)
    H1 = at("H1", [128, 16 * HT], BF16, off_h1)
    HID = at("HID", [128, NCH * HT], BF16, off_hid)
    GR = at("GR", [128, 6 * 2048], F32, off_hid)
    GRB = at("GRB", [128, 6 * 2048], BF16, off_hid)
    Q1R = at("Q1R", [128, 2 * D], BF16, off_qr)
    Q2R = at("Q2R", [128, 2 * D], BF16, off_qr + 8192)
    Q3S = at("Q3S", [128, 2 * D], BF16, off_qr + 16384)
    QC = at("QC", [128, 2 * 5632], BF16, off_qr)
    W1ROWF = at("W1ROWF", [1, D], F32, off_qr)
    EXPA = at("EXPA", [128, 2048], BF16, off_scr)
    EXPC = at("EXPC", [128, 2048], BF16, off_scr + 4096)
    C2ROW = at("C2ROW", [128, 2048], BF16, off_scr + 8192)
    VROW = at("VROW", [1, HT], F32, off_scr)
    MROWF = at("MROWF", [1, HT], F32, off_scr + 4096)
    RROW = at("RROW", [1, HT], BF16, off_scr + 8192)
    BROW = at("BROW", [1, HT], BF16, off_scr + 10240)
    STAGEB = at("STAGEB", [128, 512], F32, off_scr)
    RB = at("RB", [128, HT], BF16, off_scr + 12288)
    SIG = at("SIG", [128, HT], BF16, off_scr + 12288)
    BB = at("BB", [128, HT], BF16, off_scr + 14336)
    SQ = at("SQ", [128, HT], BF16, off_scr + 16384)
    TNORM = at("TNORM", [128, HT], BF16, off_scr + 16384)
    MRING = at("MRING", [128, 17 * 128], BF16, off_qr + 8192)
    STAGEA = at("STAGEA", [128, 512], F32, off_scr + 18432)
    W1ROWBF = at("W1ROWBF", [1, D], BF16, off_scr + 8192)

    DDIAG = nc.alloc_sbuf_tensor("DDIAG", [128, 128], F32)
    DOFF = nc.alloc_sbuf_tensor("DOFF", [128, 128], F32)
    TRI = nc.alloc_sbuf_tensor("TRI", [128, 128], BF16)
    OFFS = nc.alloc_sbuf_tensor("OFFS", [128, 16], F32)
    ONESRB = nc.alloc_sbuf_tensor("ONESRB", [1, 128], BF16)
    ONESCB = nc.alloc_sbuf_tensor("ONESCB", [128, 1], BF16)
    ONESRF = nc.alloc_sbuf_tensor("ONESRF", [1, 128], F32)
    ONESCF = nc.alloc_sbuf_tensor("ONESCF", [128, 1], F32)
    W0C = nc.alloc_sbuf_tensor("W0C", [128, 16], F32)
    BIASC = nc.alloc_sbuf_tensor("BIASC", [128, 16], F32)
    LNDV = nc.alloc_sbuf_tensor("LNDV", [128, 2], F32)
    LB0 = nc.alloc_sbuf_tensor("LB0", [128, 16], F32)
    LB1 = nc.alloc_sbuf_tensor("LB1", [128, 16], F32)
    GACC = nc.alloc_sbuf_tensor("GACC", [128, 4], F32)
    GACCT = nc.alloc_sbuf_tensor("GACCT", [128, 1], F32)
    GACCS = nc.alloc_sbuf_tensor("GACCS", [128, 132], F32)
    GDUM = nc.alloc_sbuf_tensor("GDUM", [128, 64], BF16)
    RG = nc.alloc_sbuf_tensor("RG", [128, 3], F32)
    DVS = nc.alloc_sbuf_tensor("DVS", [1, 8], F32)
    GROW = nc.alloc_sbuf_tensor("GROW", [1, 16], F32)
    W1B = nc.alloc_sbuf_tensor("W1B", [128, D], BF16)
    EPSC = nc.alloc_sbuf_tensor("EPSC", [128, 1], F32)
    assert nc.sbuf_bytes_remaining > 1024, nc.sbuf_bytes_remaining

    P0 = nc.alloc_psum_tensor("P0", [128, HT], F32)
    P1 = nc.alloc_psum_tensor("P1", [128, HT], F32)
    P2 = nc.alloc_psum_tensor("P2", [128, HT], F32)
    P3 = nc.alloc_psum_tensor("P3", [128, HT], F32)

    S = Sched()

    def d(eng, sem, out, in_, waits=()):
        return S.dma(eng, sem, (lambda e, out=out, in_=in_: e.dma_start(out=out, in_=in_)),
                     waits)

    _nold = [0]

    def mm_nold(e, out, lhsT, rhs, start, stop):
        with nc.discard():
            x = e.matmul(out, lhsT, rhs, start=start, stop=stop).ins
        _nold[0] += 1
        y = mybir.InstMatmult(
            name=f"mmnl{_nold[0]}", engine=x.engine, ins=list(x.ins),
            outs=list(x.outs),
            replication_resolution=x.replication_resolution,
            replication_shift_amnt=x.replication_shift_amnt,
            replication_num_rows=x.replication_num_rows,
            start_tensor_calc=x.start_tensor_calc,
            stop_tensor_calc=x.stop_tensor_calc,
            ldweights=False)
        return e.add_instruction(y)

    for out, in_ in ((DDIAG[:], c_ddiag), (DOFF[:], c_doff), (TRI[:], c_tri),
                     (OFFS[:], c_offs), (ONESRB[:], c_ones_r_bf), (ONESCB[:], c_ones_c_bf),
                     (ONESRF[:], c_ones_r_f), (ONESCF[:], c_ones_c_f), (W0C[:], w0c_d),
                     (BIASC[:], biasc_d), (DVS[0:1, 0:2], dv_d), (W1ROWF[:], w1row_d)):
        d("sync", "qs", out, in_)
    q_setup = S.cnt["qs"]

    def x_dma(h):
        in_ = x_fm[:, h * HT:(h + 1) * HT].rearrange("(c p) n -> p c n", p=128)
        out = XB[:].rearrange("p (c n) -> p c n", c=16)
        return S.dma("gpsimd", "qx", lambda e, o=out, i=in_: e.dma_start(out=o, in_=i))

    qg_x0 = x_dma(0)

    ev_z = S.emit("vector", lambda e: e.memset(GACCS[:], 0.0), inc="ev")
    ev_z = S.emit("vector", lambda e: e.memset(GACC[:], 0.0), waits=[("ev", ev_z)], inc="ev")
    ev_eps = S.emit("vector", lambda e: e.memset(EPSC[:], LN_EPS), inc="ev")
    gjobs = [(mat, sl) for sl in range(NCH) for mat in range(3)]
    gstate = {"next": 0, "cons": {}, "dmas": {}, "pend": []}
    gv_done = {}
    gsrcs = (w1t, w2t, w3t)
    gq_eng = {0: ("gpsimd", "qg1"), 1: ("gpsimd", "qg2"), 2: ("gpsimd", "qg3")}

    def _emit_gamma_consumer(mat, sl, wait_cnt):
        sem = gq_eng[mat][1]
        use_v = (sl % 2 == 1)
        slot = mat * 2 + (sl % 2)
        seg = GRB[:, slot * 2048:(slot + 1) * 2048]
        col = GACCS[:, mat * NCH + sl:mat * NCH + sl + 1]
        if use_v:
            cc = S.emit("vector",
                        lambda e, seg=seg, col=col: e.reduce_sum(
                            col, seg, axis=mybir.AxisListType.X,
                            apply_absolute_value=True),
                        waits=[(sem, wait_cnt), ("ev", ev_z)], inc="ev")
            gstate["cons"][(mat, sl)] = ("ev", cc)
        else:
            cc = S.emit("scalar",
                        lambda e, seg=seg, col=col: e.activation(
                            seg, seg, AF.Abs, accum_out=col),
                        waits=[(sem, wait_cnt), ("ev", ev_z)], inc="es")
            gstate["cons"][(mat, sl)] = ("es", cc)

    def emit_gamma_jobs(n):
        while n > 0 and gstate["next"] < len(gjobs):
            idx = gstate["next"]
            gstate["next"] += 1
            n -= 1
            mat, sl = gjobs[idx]
            eng, sem = gq_eng[mat]
            w = []
            if (mat, sl - 2) in gstate["cons"]:
                w.append(gstate["cons"][(mat, sl - 2)])
            slot = mat * 2 + (sl % 2)
            seg = GRB[:, slot * 2048:(slot + 1) * 2048]
            cq = S.dma(eng, sem,
                       lambda e, seg=seg, mat=mat, sl=sl: e.dma_start(
                           out=seg.rearrange("p (c n) -> p c n", c=16),
                           in_=gsrcs[mat][sl]),
                       waits=w)
            gstate["dmas"][(mat, sl)] = cq
            if sl >= 1:
                _emit_gamma_consumer(mat, sl - 1, cq)

    def finish_gamma_scan():
        emit_gamma_jobs(len(gjobs))
        for mat in range(3):
            eng, sem = gq_eng[mat]
            cq = S.dma(eng, sem,
                       lambda e: e.dma_start(out=GDUM[:], in_=c_tri[:, 0:64]))
            _emit_gamma_consumer(mat, NCH - 1, cq)
        p1 = S.emit("scalar", lambda e: e.copy(GACCT[:], GACCS[:, 0:1]), inc="es")
        p2 = S.emit("scalar", lambda e: e.copy(GACCT[:], GACCS[:, 1:2]),
                    waits=[("es", p1)], inc="es")
        for mat in range(3):
            last = [gstate["cons"][(mat, sl)] for sl in range(NCH)] + [("es", p2)]
            gv_done[mat] = S.emit(
                "vector",
                lambda e, mat=mat: e.reduce_sum(
                    GACC[:, mat:mat + 1], GACCS[:, mat * NCH:(mat + 1) * NCH],
                    axis=mybir.AxisListType.X),
                waits=last, inc="ev")

    ev_dv = S.emit("vector",
                   lambda e: e.tensor_scalar(DVS[0:1, 0:2], DVS[0:1, 0:2], 0.9, 1.0,
                                             ALU.max, ALU.min),
                   waits=[("qs", q_setup)], inc="ev")
    es_ln = S.emit("scalar",
                   lambda e: e.activation(DVS[0:1, 2:4], DVS[0:1, 0:2], AF.Ln),
                   waits=[("ev", ev_dv)], inc="es")
    ep_dvb = S.emit("tensor",
                    lambda e: e.matmul(P0[:, 0:2], ONESRF[:], DVS[0:1, 2:4],
                                       start=True, stop=True),
                    waits=[("es", es_ln)], inc="ep")
    es_dvc = S.emit("scalar", lambda e: e.copy(LNDV[:], P0[:, 0:2]),
                    waits=[("ep", ep_dvb)], inc="es")
    ev_lb = None
    for i, LB in enumerate((LB0, LB1)):
        ev_lb = S.emit("vector",
                       lambda e, LB=LB, i=i: e.tensor_scalar(LB[:], OFFS[:],
                                                             LNDV[:, i:i + 1], None, ALU.mult),
                       waits=[("es", es_dvc)], inc="ev")

    def setup_half(h):
        if h == 1:
            qW = d("sync", "qs", W1ROWF[:], w1row_d)
            w = [("qs", qW)]
        else:
            w = [("qs", q_setup)]
        es_wr = S.emit("scalar", lambda e: e.copy(W1ROWBF[:], W1ROWF[:]),
                       waits=w, inc="es")
        ep_w = None
        for half in range(2):
            P = (P0, P1)[half]
            for s in range(2):
                first = (half == 0 and s == 0)
                ep_w = S.emit("tensor",
                              lambda e, P=P, half=half, s=s, first=first: (
                                  e.matmul if first else (
                                      lambda o, l, r, start, stop: mm_nold(
                                          e, o, l, r, start, stop)))(
                                  P[:, s * 512:(s + 1) * 512], ONESRB[:],
                                  W1ROWBF[0:1, half * 1024 + s * 512:half * 1024 + (s + 1) * 512],
                                  start=True, stop=True),
                              waits=[("es", es_wr)], inc="ep")
        es_wb = None
        for half in range(2):
            P = (P0, P1)[half]
            es_wb = S.emit("scalar",
                           lambda e, P=P, half=half: e.copy(
                               W1B[:, half * 1024:(half + 1) * 1024], P[:]),
                           waits=[("ep", ep_w)], inc="es")
        es_exp = None
        for m in range(16):
            DDm = DDIAG if m == 0 else DOFF
            es_exp = S.emit("scalar",
                            lambda e, m=m, DDm=DDm: e.activation(
                                EXPA[:, m * 128:(m + 1) * 128], DDm[:], AF.Exp,
                                bias=LB0[:, m:m + 1], scale=LNDV[:, 0:1]),
                            waits=[("ev", ev_lb), ("es", es_wb)], inc="es")
            es_exp = S.emit("scalar",
                            lambda e, m=m, DDm=DDm: e.activation(
                                EXPC[:, m * 128:(m + 1) * 128], DDm[:], AF.Exp,
                                bias=LB1[:, m:m + 1], scale=LNDV[:, 1:2]),
                            inc="es")
        return es_exp, es_wb

    def phase_A(h, es_setup, es_wb, qg_x):
        tile_build = {}
        tile_done = {}
        evac = {}
        stats_done = {}
        gctr = [0]

        def mixer_j(j):
            g = gctr[0]
            if USE_BCAST_AP:
                n = j + 1
                in0 = EXPC[:, 0:n * 128].rearrange("p (m q) -> p m q", m=n)
                in1 = W1B[:, j * 128:(j + 1) * 128].rearrange(
                    "p (o q) -> p o q", o=1).broadcast_to([128, n, 128])
                o3 = C2ROW[:, 0:n * 128].rearrange("p (m q) -> p m q", m=n)
                S.emit("vector",
                       lambda e, o3=o3, in0=in0, in1=in1: e.tensor_tensor(o3, in0, in1,
                                                                          ALU.mult),
                       waits=[("es", es_setup)])
            else:
                for m in range(j + 1):
                    S.emit("vector",
                           lambda e, m=m, j=j: e.tensor_tensor(
                               C2ROW[:, m * 128:(m + 1) * 128],
                               EXPC[:, m * 128:(m + 1) * 128],
                               W1B[:, j * 128:(j + 1) * 128], ALU.mult),
                           waits=[("es", es_setup)])
            P = (P0, P1)[j % 2]
            for k in range(j + 1):
                m = j - k
                slot = g % 17
                w = [("es", es_setup)]
                if g - 17 in tile_done:
                    w.append(("ep", tile_done[g - 17]))
                cv = S.emit("vector",
                            lambda e, m=m, k=k, slot=slot: e.scalar_tensor_tensor(
                                MRING[:, slot * 128:(slot + 1) * 128],
                                EXPA[:, m * 128:(m + 1) * 128], W0C[:, k:k + 1],
                                C2ROW[:, m * 128:(m + 1) * 128], ALU.mult, ALU.add),
                            waits=w, inc="ev")
                if k == j:
                    cv = S.emit("vector",
                                lambda e, slot=slot: e.tensor_tensor(
                                    MRING[:, slot * 128:(slot + 1) * 128],
                                    MRING[:, slot * 128:(slot + 1) * 128], TRI[:], ALU.mult),
                                inc="ev")
                tile_build[g] = cv
                g += 1
            for k in range(j + 1):
                gk = g - (j + 1) + k
                slot = gk % 17
                w = []
                if k == 0:
                    w = [("ev", tile_build[g - 1]), ("qx", qg_x), ("es", es_wb)]
                    if (j - 2) in evac:
                        w.append(("ev", evac[j - 2]))
                S.emit("tensor",
                       lambda e, P=P, slot=slot, k=k, j=j: e.matmul(
                           P[:, 0:512], MRING[:, slot * 128:(slot + 1) * 128],
                           XB[:, k * HT:k * HT + 512], start=(k == 0), stop=(k == j)),
                       waits=w)
                cp = S.emit("tensor",
                            lambda e, P=P, slot=slot, k=k, j=j: mm_nold(
                                e, P[:, 512:1024], MRING[:, slot * 128:(slot + 1) * 128],
                                XB[:, k * HT + 512:(k + 1) * HT], (k == 0), (k == j)),
                            inc="ep")
                tile_done[gk] = cp
            gctr[0] = g

        def post_j(j):
            P = (P0, P1)[j % 2]
            cv = S.emit("vector",
                        lambda e, P=P, j=j: e.scalar_tensor_tensor(
                            H1[:, j * HT:(j + 1) * HT], P[:], BIASC[:, j:j + 1],
                            XB[:, j * HT:(j + 1) * HT], ALU.add, ALU.add),
                        waits=[("ep", tile_done[(j + 1) * (j + 2) // 2 - 1]),
                               ("qx", qg_x)], inc="ev")
            evac[j] = cv
            w = [("ev", cv)]
            if j - 1 in stats_done:
                w.append(("ep", stats_done[j - 1]))
            cs = S.emit("scalar",
                        lambda e, j=j: e.activation(SQ[:], H1[:, j * HT:(j + 1) * HT],
                                                    AF.Square),
                        waits=w, inc="es")
            cp2 = None
            for s in range(2):
                S.emit("tensor",
                       lambda e, j=j, s=s: (e.matmul(
                           P2[0:1, s * 512:(s + 1) * 512], ONESCB[:],
                           H1[:, j * HT + s * 512:j * HT + (s + 1) * 512],
                           start=(j == 0), stop=(j == 15)) if s == 0 else mm_nold(
                           e, P2[0:1, s * 512:(s + 1) * 512], ONESCB[:],
                           H1[:, j * HT + s * 512:j * HT + (s + 1) * 512],
                           (j == 0), (j == 15))),
                       waits=[("ev", cv)])
                cp2 = S.emit("tensor",
                             lambda e, j=j, s=s: mm_nold(
                                 e, P3[0:1, s * 512:(s + 1) * 512], ONESCB[:],
                                 SQ[:, s * 512:(s + 1) * 512],
                                 (j == 0), (j == 15)),
                             waits=[("es", cs)], inc="ep")
            stats_done[j] = cp2

        for j in range(16):
            mixer_j(j)
            if j >= 1:
                post_j(j - 1)
            if h == 0:
                emit_gamma_jobs(9)
        post_j(15)
        return stats_done[15], evac

    def ln_finalize(stats_ep, last_z_ev):
        c1 = S.emit("vector",
                    lambda e: e.tensor_scalar(MROWF[:], P2[0:1, :], 1.0 / D, 0.0,
                                              ALU.mult, ALU.add),
                    waits=[("ep", stats_ep), ("ev", last_z_ev)], inc="ev")
        S.emit("vector",
               lambda e: e.tensor_tensor(VROW[:], MROWF[:], MROWF[:], ALU.mult))
        c3 = S.emit("vector",
                    lambda e: e.scalar_tensor_tensor(VROW[:], P3[0:1, :], 1.0 / D,
                                                     VROW[:], ALU.mult, ALU.subtract),
                    inc="ev")
        S.emit("scalar",
               lambda e: e.activation(VROW[:], VROW[:], AF.Ln, bias=EPSC[0:1, 0:1]),
               waits=[("ev", c3), ("ev", ev_eps)])
        c5 = S.emit("scalar",
                    lambda e: e.activation(RROW[:], VROW[:], AF.Exp, scale=-0.5),
                    inc="es")
        c6 = S.emit("vector",
                    lambda e: e.scalar_tensor_tensor(BROW[:], MROWF[:], -1.0, RROW[:],
                                                     ALU.mult, ALU.mult),
                    waits=[("es", c5)], inc="ev")
        cp = None
        for s in range(2):
            S.emit("tensor",
                   lambda e, s=s: (e.matmul(P0[:, s * 512:(s + 1) * 512], ONESRB[:],
                                            RROW[0:1, s * 512:(s + 1) * 512],
                                            start=True, stop=True) if s == 0 else
                                   mm_nold(e, P0[:, s * 512:(s + 1) * 512], ONESRB[:],
                                           RROW[0:1, s * 512:(s + 1) * 512], True, True)),
                   waits=[("ev", c6)])
            cp = S.emit("tensor",
                        lambda e, s=s: mm_nold(
                            e, P1[:, s * 512:(s + 1) * 512], ONESRB[:],
                            BROW[0:1, s * 512:(s + 1) * 512], True, True),
                        inc="ep")
        S.emit("scalar", lambda e: e.copy(RB[:], P0[:]), waits=[("ep", cp)])
        c8 = S.emit("scalar", lambda e: e.copy(BB[:], P1[:]), inc="es")
        return c8

    def gamma_finalize():
        w = [("ev", gv_done[m]) for m in range(3)]
        cp = S.emit("tensor",
                    lambda e: e.matmul(P2[0:1, 0:3], ONESCF[:], GACC[:, 0:3],
                                       start=True, stop=True),
                    waits=w, inc="ep")
        c1 = S.emit("vector",
                    lambda e: e.tensor_scalar(GROW[0:1, 0:3], P2[0:1, 0:3], 1.0 / NW,
                                              G_EPS, ALU.mult, ALU.add),
                    waits=[("ep", cp)], inc="ev")
        c2 = S.emit("vector",
                    lambda e: e.reciprocal(GROW[0:1, 4:7], GROW[0:1, 0:3]),
                    waits=[("ev", c1)], inc="ev")
        cp2 = S.emit("tensor",
                     lambda e: e.matmul(P2[:, 0:3], ONESRF[:], GROW[0:1, 4:7],
                                        start=True, stop=True),
                     waits=[("ev", c2)], inc="ep")
        c3 = S.emit("scalar", lambda e: e.copy(RG[:], P2[:, 0:3]),
                    waits=[("ep", cp2)], inc="es")
        return c3

    def normalize_bf16(ln_es):
        last = None
        for j in range(16):
            S.emit("vector",
                   lambda e, j=j: e.tensor_tensor(
                       H1[:, j * HT:(j + 1) * HT], H1[:, j * HT:(j + 1) * HT],
                       RB[:], ALU.mult),
                   waits=[("es", ln_es)])
            last = S.emit("vector",
                          lambda e, j=j: e.tensor_tensor(
                              H1[:, j * HT:(j + 1) * HT], H1[:, j * HT:(j + 1) * HT],
                              BB[:], ALU.add),
                          inc="ev")
        return last

    def normalize_store(ln_es, h):
        stage_dma = {}
        for j in range(16):
            S.emit("vector",
                   lambda e, j=j: e.tensor_tensor(
                       TNORM[:], H1[:, j * HT:(j + 1) * HT], RB[:], ALU.mult),
                   waits=[("es", ln_es)])
            for s in range(2):
                u = j * 2 + s
                STG = (STAGEA, STAGEB)[u % 2]
                w = []
                if u >= 2:
                    w.append(("qST", S.cnt["qST"]))
                cv = S.emit("vector",
                            lambda e, s=s, STG=STG: e.tensor_tensor(
                                STG[:], TNORM[:, s * 512:(s + 1) * 512],
                                BB[:, s * 512:(s + 1) * 512], ALU.add),
                            waits=w, inc="ev")
                stage_dma[u] = S.dma("sync", "qST",
                                     lambda e, j=j, s=s, STG=STG: e.dma_start(
                                         out=out_fm[j * 128:(j + 1) * 128,
                                                    h * HT + s * 512:h * HT + (s + 1) * 512],
                                         in_=STG[:]),
                                     waits=[("ev", cv)])

    def phase_B(h, rg_es, norm_ev):
        mult_done = {}
        sig_done = {}
        pe_done = {}
        gp_done = {}
        q_ready_d = {}
        QMAT = {0: (L1, Q1R, w1t, q1c, "sync", "qL1", "qc1"),
                1: (L2, Q2R, w2t, q2c, "scalar", "qL2", "qc2"),
                2: (L3, Q3S, w3t, q3c, "gpsimd", "qL3", "qc3")}
        pend = {}

        def quant_consumer(mat, u, wait_cnt):
            L, QOUT, src_t, qcache, eng_q, semL, semC = QMAT[mat]
            hb, piece = u // 2, u % 2
            hs = hb % 2
            slot = u % 2
            seg = L[:, slot * 1024:(slot + 1) * 1024]
            cs = S.emit("scalar",
                        lambda e, seg=seg, mat=mat: e.activation(
                            seg, seg, AF.Copy, bias=MAGIC, scale=RG[:, mat:mat + 1]),
                        waits=[(semL, wait_cnt), ("es", rg_es)], inc="es")
            S.emit("vector",
                   lambda e, seg=seg: e.tensor_scalar(seg, seg, CM1, CP1,
                                                      ALU.max, ALU.min),
                   waits=[("es", cs)])
            w = [(semC, S.cnt[semC])]
            if hb - 2 in pe_done and mat < 2:
                w.append(("ep", pe_done[hb - 2]))
            qcol = hs * D + piece * 1024
            cg = S.emit("vector",
                        lambda e, seg=seg, QOUT=QOUT, qcol=qcol: e.tensor_scalar(
                            QOUT[:, qcol:qcol + 1024], seg, MAGIC, None, ALU.subtract),
                        waits=w, inc="ev")
            gp_done[(mat, u)] = cg
            if piece == 1:
                co = (q3c[hb].rearrange("p c n -> p (c n)") if mat == 2
                      else qcache[hb])
                S.dma(eng_q, semC,
                      lambda e, co=co, QOUT=QOUT, hs=hs: e.dma_start(
                          out=co, in_=QOUT[:, hs * D:(hs + 1) * D]),
                      waits=[("ev", cg)])
            return cg

        def quant_dma(mat, hb, piece):
            L, QOUT, src_t, qcache, eng_q, semL, semC = QMAT[mat]
            u = 2 * hb + piece
            slot = u % 2
            seg = L[:, slot * 1024:(slot + 1) * 1024]
            w = []
            if (mat, u - 2) in gp_done:
                w.append(("ev", gp_done[(mat, u - 2)]))
            cq = S.dma(eng_q, semL,
                       lambda e, seg=seg, src=src_t[hb], piece=piece: e.dma_start(
                           out=seg.rearrange("p (c n) -> p c n", c=8),
                           in_=src[:, piece * 8:(piece + 1) * 8, :]),
                       waits=w)
            if pend.get(mat) is not None:
                quant_consumer(mat, pend[mat], cq)
            pend[mat] = u

        def flush_pending():
            for mat in range(3):
                if pend.get(mat) is None:
                    continue
                L, QOUT, src_t, qcache, eng_q, semL, semC = QMAT[mat]
                cq = S.dma(eng_q, semL,
                           lambda e: e.dma_start(out=GDUM[:], in_=c_tri[:, 0:64]))
                quant_consumer(mat, pend[mat], cq)
                pend[mat] = None

        def produce(hb):
            hs = hb % 2
            if h == 0:
                for mat in range(3):
                    quant_dma(mat, hb, 0)
                    quant_dma(mat, hb, 1)
            else:
                w = []
                if hb - 2 in pe_done:
                    w.append(("ep", pe_done[hb - 2]))
                d2 = dict((("sync", "qR1"), ("scalar", "qR2")))
                S.dma("sync", "qR1",
                      lambda e, hs=hs, hb=hb: e.dma_start(
                          out=Q1R[:, hs * D:(hs + 1) * D], in_=q1c[hb]), w)
                S.dma("scalar", "qR2",
                      lambda e, hs=hs, hb=hb: e.dma_start(
                          out=Q2R[:, hs * D:(hs + 1) * D], in_=q2c[hb]), w)

        def consume(hb):
            hs = hb % 2
            if h == 0:
                q_ready = [("ev", gp_done[(0, 2 * hb + 1)]),
                           ("ev", gp_done[(1, 2 * hb + 1)])]
            else:
                q_ready = [("qR1", S.cnt["qR1"]), ("qR2", S.cnt["qR2"])]
            PA, PB = (P0, P1) if hs == 0 else (P2, P3)
            w = list(q_ready) + [("ev", norm_ev)]
            if hb - 2 in mult_done:
                w.append(("ev", mult_done[hb - 2]))
            cp = None
            for QR, PX in ((Q1R, PA), (Q2R, PB)):
                for dc in range(NCD):
                    for s in range(2):
                        last = (QR is Q2R and dc == NCD - 1 and s == 1)
                        cp = S.emit("tensor",
                                    lambda e, QR=QR, PX=PX, hs=hs, dc=dc, s=s: (e.matmul(
                                        PX[:, s * 512:(s + 1) * 512],
                                        QR[:, hs * D + dc * 128:hs * D + (dc + 1) * 128],
                                        H1[:, dc * HT + s * 512:dc * HT + (s + 1) * 512],
                                        start=(dc == 0), stop=(dc == NCD - 1)) if s == 0
                                        else mm_nold(
                                        e, PX[:, s * 512:(s + 1) * 512],
                                        QR[:, hs * D + dc * 128:hs * D + (dc + 1) * 128],
                                        H1[:, dc * HT + s * 512:dc * HT + (s + 1) * 512],
                                        (dc == 0), (dc == NCD - 1))),
                                    waits=(w if QR is Q1R and dc == 0 and s == 0 else ()),
                                    inc=("ep" if last else None))
            pe_done[hb] = cp
            w = [("ep", cp)]
            if hb - 1 in mult_done:
                w.append(("ev", mult_done[hb - 1]))
            cs = S.emit("scalar",
                        lambda e, PA=PA: e.activation(SIG[:], PA[:], AF.Sigmoid),
                        waits=w, inc="es")
            sig_done[hb] = cs
            cv = S.emit("vector",
                        lambda e, PB=PB, hb=hb: e.tensor_tensor(
                            HID[:, hb * HT:(hb + 1) * HT], PB[:], SIG[:], ALU.mult),
                        waits=[("es", cs)], inc="ev")
            mult_done[hb] = cv

        for hb in range(NCH):
            produce(hb)
            if hb >= 1:
                consume(hb - 1)
        if h == 0:
            flush_pending()
        consume(NCH - 1)
        return mult_done[NCH - 1]

    def phase_C(h, hid_ev):
        evac = {}
        stats_done = {}
        pe_f = {}

        def col_dma(dco):
            slot = dco % 2
            w = []
            if dco - 2 in pe_f:
                w.append(("ep", pe_f[dco - 2]))
            S.dma("sync", "qCC",
                  lambda e, slot=slot, dco=dco: e.dma_start(
                      out=QC[:, slot * 5632:(slot + 1) * 5632].rearrange(
                          "p (a b) -> p a b", a=NCH),
                      in_=q3c[:, :, dco, :].rearrange("a p b -> p a b")),
                  waits=w)

        def cpe(dco):
            slot = dco % 2
            P = (P0, P1)[slot]
            w = [("qCC", S.cnt["qCC"]), ("ev", hid_ev)]
            if dco - 2 in evac:
                w.append(("ev", evac[dco - 2]))
            cp = None
            for k in range(NCH):
                for s in range(2):
                    last = (k == NCH - 1 and s == 1)
                    cp = S.emit("tensor",
                                lambda e, P=P, slot=slot, k=k, s=s: (e.matmul(
                                    P[:, s * 512:(s + 1) * 512],
                                    QC[:, slot * 5632 + k * 128:slot * 5632 + (k + 1) * 128],
                                    HID[:, k * HT + s * 512:k * HT + (s + 1) * 512],
                                    start=(k == 0), stop=(k == NCH - 1)) if s == 0
                                    else mm_nold(
                                    e, P[:, s * 512:(s + 1) * 512],
                                    QC[:, slot * 5632 + k * 128:slot * 5632 + (k + 1) * 128],
                                    HID[:, k * HT + s * 512:k * HT + (s + 1) * 512],
                                    (k == 0), (k == NCH - 1))),
                                waits=(w if k == 0 and s == 0 else ()),
                                inc=("ep" if last else None))
            pe_f[dco] = cp

        def cpost(dco):
            P = (P0, P1)[dco % 2]
            cv = S.emit("vector",
                        lambda e, P=P, dco=dco: e.tensor_tensor(
                            H1[:, dco * HT:(dco + 1) * HT], P[:],
                            H1[:, dco * HT:(dco + 1) * HT], ALU.add),
                        waits=[("ep", pe_f[dco])], inc="ev")
            evac[dco] = cv
            w = [("ev", cv)]
            if dco - 1 in stats_done:
                w.append(("ep", stats_done[dco - 1]))
            cs = S.emit("scalar",
                        lambda e, dco=dco: e.activation(SQ[:], H1[:, dco * HT:(dco + 1) * HT],
                                                        AF.Square),
                        waits=w, inc="es")
            cp2 = None
            for s in range(2):
                S.emit("tensor",
                       lambda e, dco=dco, s=s: (e.matmul(
                           P2[0:1, s * 512:(s + 1) * 512], ONESCB[:],
                           H1[:, dco * HT + s * 512:dco * HT + (s + 1) * 512],
                           start=(dco == 0), stop=(dco == NCD - 1)) if s == 0 else
                           mm_nold(
                           e, P2[0:1, s * 512:(s + 1) * 512], ONESCB[:],
                           H1[:, dco * HT + s * 512:dco * HT + (s + 1) * 512],
                           (dco == 0), (dco == NCD - 1))),
                       waits=[("ev", cv)])
                cp2 = S.emit("tensor",
                             lambda e, dco=dco, s=s: mm_nold(
                                 e, P3[0:1, s * 512:(s + 1) * 512], ONESCB[:],
                                 SQ[:, s * 512:(s + 1) * 512],
                                 (dco == 0), (dco == NCD - 1)),
                             waits=[("es", cs)], inc="ep")
            stats_done[dco] = cp2

        for dco in range(NCD):
            col_dma(dco)
            if dco >= 1:
                cpe(dco - 1)
            if dco >= 2:
                cpost(dco - 2)
        cpe(NCD - 1)
        cpost(NCD - 2)
        cpost(NCD - 1)
        ln_es = ln_finalize(stats_done[NCD - 1], evac[NCD - 1])
        normalize_store(ln_es, h)

    es_setup, es_wb = setup_half(0)
    stats_ep, evacA = phase_A(0, es_setup, es_wb, qg_x0)
    ln_es = ln_finalize(stats_ep, evacA[15])
    norm_ev = normalize_bf16(ln_es)
    if dump_h1:
        d("sync", "qs", dbg_h1[0], H1[:], waits=[("ev", norm_ev)])
    if phases >= 2:
        finish_gamma_scan()
        rg_es = gamma_finalize()
        S.barrier()
        hid_ev = phase_B(0, rg_es, norm_ev)
        if dump_hid:
            d("sync", "qs", dbg_hid[0], HID[:], waits=[("ev", hid_ev)])
        if dump_q:
            S.barrier()
            d("sync", "qs", dbg_q1[:], q1c)
            d("sync", "qs", dbg_q2[:], q2c)
            d("sync", "qs", dbg_rg, RG[:])
            d("sync", "qs", dbg_ga, GACC[:])
    if phases >= 3:
        S.barrier()
        phase_C(0, hid_ev)
        S.barrier()
        qg_x1 = x_dma(1)
        es_setup2, es_wb2 = setup_half(1)
        stats_ep2, evacA2 = phase_A(1, es_setup2, es_wb2, qg_x1)
        ln_es2 = ln_finalize(stats_ep2, evacA2[15])
        norm_ev2 = normalize_bf16(ln_es2)
        if dump_h1:
            d("sync", "qs", dbg_h1[1], H1[:], waits=[("ev", norm_ev2)])
        S.barrier()
        hid_ev2 = phase_B(1, rg_es, norm_ev2)
        if dump_hid:
            d("sync", "qs", dbg_hid[1], HID[:], waits=[("ev", hid_ev2)])
        S.barrier()
        phase_C(1, hid_ev2)

    qfin = [(n, S.cnt[n]) for n in S.cnt if n.startswith("q")]
    S.emit("sync", lambda e: e.nop(), waits=qfin)

    with contextlib.ExitStack() as stack:
        sems = {n: stack.enter_context(nc.semaphore(n)) for n in S.cnt}
        block = stack.enter_context(nc.Block())

        def make_body(eng_name):
            def body(e):
                for waits, fn, inc, amt in S.ops[eng_name]:
                    for s, v in waits:
                        e.wait_ge(sems[s], v)
                    ins = fn(e)
                    if inc is not None:
                        ins.then_inc(sems[inc], amt)
            return body

        for eng_name in ENGS:
            getattr(block, eng_name)(make_body(eng_name))
    return nc


def _host_layout(L0_attn_x, L0_attn_weight, L0_attn_bias, L0_attn_decay_value,
                 L0_ffn_w1, L0_ffn_w2, L0_ffn_w3):
    f32 = np.float32
    x2 = np.ascontiguousarray(np.asarray(L0_attn_x, f32).reshape(8 * TOK, D))
    w1t = np.ascontiguousarray(np.asarray(L0_ffn_w1, f32).reshape(NCH, 128, NCD, 128)
                               .transpose(0, 3, 2, 1))
    w2t = np.ascontiguousarray(np.asarray(L0_ffn_w2, f32).reshape(NCH, 128, NCD, 128)
                               .transpose(0, 3, 2, 1))
    w3t = np.ascontiguousarray(np.asarray(L0_ffn_w3, f32).reshape(NCD, 128, NCH, 128)
                               .transpose(2, 3, 0, 1))
    wt = np.asarray(L0_attn_weight, f32)
    w0c = np.ascontiguousarray(wt[0].reshape(NCD, 128).T)
    w1row = np.ascontiguousarray(wt[1].reshape(1, D))
    biasc = np.ascontiguousarray(np.asarray(L0_attn_bias, f32).reshape(NCD, 128).T)
    dv = np.ascontiguousarray(np.asarray(L0_attn_decay_value, f32).reshape(1, 2))

    ii = np.arange(128, dtype=f32)
    dmat = ii[None, :] - ii[:, None]
    shared = {
        "c_ddiag": np.ascontiguousarray(np.maximum(dmat, 0.0).astype(f32)),
        "c_doff": np.ascontiguousarray(dmat.astype(f32)),
        "c_tri": np.ascontiguousarray((dmat >= 0).astype(ml_dtypes.bfloat16)),
        "c_offs": np.ascontiguousarray(
            np.broadcast_to(128.0 * np.arange(16, dtype=f32), (128, 16))),
        "c_ones_r_bf": np.ones((1, 128), ml_dtypes.bfloat16),
        "c_ones_c_bf": np.ones((128, 1), ml_dtypes.bfloat16),
        "c_ones_r_f": np.ones((1, 128), f32),
        "c_ones_c_f": np.ones((128, 1), f32),
        "w1t": w1t, "w2t": w2t, "w3t": w3t, "w0c": w0c, "biasc": biasc,
        "dv": dv, "w1row": w1row,
    }
    in_maps = []
    for c in range(8):
        m = dict(shared)
        m["x_fm"] = np.ascontiguousarray(x2[c * TOK:(c + 1) * TOK, :].T)
        in_maps.append(m)
    return in_maps


def kernel(L0_attn_x, L0_attn_weight, L0_attn_bias, L0_attn_decay_value,
           L0_ffn_w1, L0_ffn_w2, L0_ffn_w3):
    in_maps = _host_layout(L0_attn_x, L0_attn_weight, L0_attn_bias,
                           L0_attn_decay_value, L0_ffn_w1, L0_ffn_w2, L0_ffn_w3)
    nc = build_program()
    res = run_bass_kernel_spmd(nc, in_maps, core_ids=list(range(8)))
    parts = [np.asarray(r["out_fm"]).T for r in res.results]
    return np.ascontiguousarray(np.concatenate(parts, axis=0)
                                .reshape(2, 8192, D)).astype(np.float32)


# version 12
# speedup vs baseline: 1.4759x; 1.4759x over previous
import contextlib
import numpy as np
import ml_dtypes
import concourse.bass as bass
import concourse.mybir as mybir
from concourse.bass_utils import run_bass_kernel_spmd

F32 = mybir.dt.float32
BF16 = mybir.dt.bfloat16
AF = mybir.ActivationFunctionType
ALU = mybir.AluOpType

MAGIC = float(np.float32(1.5 * 2 ** 23))
CM1 = float(np.float32(MAGIC - 1.0))
CP1 = float(np.float32(MAGIC + 1.0))
LN_EPS = 1e-5
G_EPS = 1e-8
NW = 5632 * 2048
D = 2048
H = 5632
NCD = 16
NCH = 44
TOK = 2048
HT = 1024
ENGS = ("sync", "gpsimd", "vector", "scalar", "tensor")

USE_BCAST_AP = True


class Sched:

    def __init__(self):
        self.ops = {e: [] for e in ENGS}
        names = ["qs", "qa", "qg", "qx", "qc1", "qc2", "qc3",
                 "ep", "ev", "es", "eg"]
        for b in ("qg1", "qg2", "qg3", "qL1", "qL2", "qL3", "qR1", "qR2",
                  "qCC", "qST"):
            names += [b + "a", b + "b"]
        self.cnt = {n: 0 for n in names}
        self.hw = {e: {} for e in ENGS}

    def emit(self, eng, fn, waits=(), inc=None, amt=1):
        w = []
        for s, v in waits:
            if v is None or v <= 0:
                continue
            assert v <= self.cnt[s], f"wait {s}>={v} but only {self.cnt[s]} scheduled"
            if self.hw[eng].get(s, 0) >= v:
                continue
            self.hw[eng][s] = v
            w.append((s, v))
        val = None
        if inc is not None:
            self.cnt[inc] += amt
            val = self.cnt[inc]
        self.ops[eng].append((tuple(w), fn, inc, amt))
        return val

    def dma(self, eng, sem, fn, waits=()):
        return self.emit(eng, fn, waits, inc=sem, amt=16)

    def barrier(self):
        snap = [(s, v) for s, v in self.cnt.items()]
        for eng, sem in (("tensor", "ep"), ("vector", "ev"),
                         ("scalar", "es"), ("gpsimd", "eg")):
            self.emit(eng, lambda e: e.nop(), waits=snap, inc=sem)
        snap2 = [(s, v) for s, v in self.cnt.items()]
        for eng in ENGS:
            self.emit(eng, lambda e: e.nop(), waits=snap2)


def build_program(phases=3, dump_h1=False, dump_hid=False, dump_q=False):
    nc = bass.Bass("TRN2", target_bir_lowering=False)

    x_fm = nc.dram_tensor("x_fm", [D, TOK], F32, kind="ExternalInput").ap()
    w1t = nc.dram_tensor("w1t", [NCH, 128, NCD, 128], F32, kind="ExternalInput").ap()
    w2t = nc.dram_tensor("w2t", [NCH, 128, NCD, 128], F32, kind="ExternalInput").ap()
    w3t = nc.dram_tensor("w3t", [NCH, 128, NCD, 128], F32, kind="ExternalInput").ap()
    c_ddiag = nc.dram_tensor("c_ddiag", [128, 128], F32, kind="ExternalInput").ap()
    c_doff = nc.dram_tensor("c_doff", [128, 128], F32, kind="ExternalInput").ap()
    c_tri = nc.dram_tensor("c_tri", [128, 128], BF16, kind="ExternalInput").ap()
    c_offs = nc.dram_tensor("c_offs", [128, 16], F32, kind="ExternalInput").ap()
    c_ones_r_bf = nc.dram_tensor("c_ones_r_bf", [1, 128], BF16, kind="ExternalInput").ap()
    c_ones_c_bf = nc.dram_tensor("c_ones_c_bf", [128, 1], BF16, kind="ExternalInput").ap()
    c_ones_r_f = nc.dram_tensor("c_ones_r_f", [1, 128], F32, kind="ExternalInput").ap()
    c_ones_c_f = nc.dram_tensor("c_ones_c_f", [128, 1], F32, kind="ExternalInput").ap()
    w0c_d = nc.dram_tensor("w0c", [128, 16], F32, kind="ExternalInput").ap()
    biasc_d = nc.dram_tensor("biasc", [128, 16], F32, kind="ExternalInput").ap()
    dv_d = nc.dram_tensor("dv", [1, 2], F32, kind="ExternalInput").ap()
    w1row_d = nc.dram_tensor("w1row", [1, D], F32, kind="ExternalInput").ap()

    out_fm = nc.dram_tensor("out_fm", [D, TOK], F32, kind="ExternalOutput").ap()

    q1c = nc.dram_tensor("q1c", [NCH, 128, D], BF16, kind="Internal").ap()
    q2c = nc.dram_tensor("q2c", [NCH, 128, D], BF16, kind="Internal").ap()
    q3c = nc.dram_tensor("q3c", [NCH, 128, NCD, 128], BF16, kind="Internal").ap()

    dbg_h1 = dbg_hid = dbg_q1 = dbg_q2 = dbg_rg = dbg_ga = None
    if dump_h1:
        dbg_h1 = nc.dram_tensor("dbg_h1", [2, 128, 16 * HT], BF16, kind="ExternalOutput").ap()
    if dump_hid:
        dbg_hid = nc.dram_tensor("dbg_hid", [2, 128, NCH * HT], BF16, kind="ExternalOutput").ap()
    if dump_q:
        dbg_q1 = nc.dram_tensor("dbg_q1", [NCH, 128, D], BF16, kind="ExternalOutput").ap()
        dbg_q2 = nc.dram_tensor("dbg_q2", [NCH, 128, D], BF16, kind="ExternalOutput").ap()
        dbg_rg = nc.dram_tensor("dbg_rg", [128, 3], F32, kind="ExternalOutput").ap()
        dbg_ga = nc.dram_tensor("dbg_ga", [128, 4], F32, kind="ExternalOutput").ap()

    REG = 32768 + 32768 + 90112 + 24576 + 20480
    base, _ = nc.bump_sbuf(REG)
    off_x, off_h1, off_hid, off_qr, off_scr = (
        base, base + 32768, base + 65536, base + 155648, base + 180224)

    def at(name, shape, dtype, off):
        return nc.alloc_sbuf_tensor_at(name, shape, dtype, offset=off)

    XB = at("XB", [128, 16 * HT], BF16, off_x)
    L1 = at("L1", [128, 2048], F32, off_x)
    L2 = at("L2", [128, 2048], F32, off_x + 8192)
    L3 = at("L3", [128, 2048], F32, off_x + 16384)
    H1 = at("H1", [128, 16 * HT], BF16, off_h1)
    HID = at("HID", [128, NCH * HT], BF16, off_hid)
    GR = at("GR", [128, 6 * 2048], F32, off_hid)
    GRB = at("GRB", [128, 6 * 2048], BF16, off_hid)
    Q1R = at("Q1R", [128, 2 * D], BF16, off_qr)
    Q2R = at("Q2R", [128, 2 * D], BF16, off_qr + 8192)
    Q3S = at("Q3S", [128, 2 * D], BF16, off_qr + 16384)
    QC = at("QC", [128, 2 * 5632], BF16, off_qr)
    W1ROWF = at("W1ROWF", [1, D], F32, off_qr)
    EXPA = at("EXPA", [128, 2048], BF16, off_scr)
    EXPC = at("EXPC", [128, 2048], BF16, off_scr + 4096)
    C2ROW = at("C2ROW", [128, 2048], BF16, off_scr + 8192)
    VROW = at("VROW", [1, HT], F32, off_scr)
    MROWF = at("MROWF", [1, HT], F32, off_scr + 4096)
    RROW = at("RROW", [1, HT], BF16, off_scr + 8192)
    BROW = at("BROW", [1, HT], BF16, off_scr + 10240)
    STAGEB = at("STAGEB", [128, 512], F32, off_scr)
    RB = at("RB", [128, HT], BF16, off_scr + 12288)
    SIG = at("SIG", [128, HT], BF16, off_scr + 12288)
    BB = at("BB", [128, HT], BF16, off_scr + 14336)
    SQ = at("SQ", [128, HT], BF16, off_scr + 16384)
    TNORM = at("TNORM", [128, HT], BF16, off_scr + 16384)
    MRING = at("MRING", [128, 17 * 128], BF16, off_qr + 8192)
    STAGEA = at("STAGEA", [128, 512], F32, off_scr + 18432)
    W1ROWBF = at("W1ROWBF", [1, D], BF16, off_scr + 8192)

    DDIAG = nc.alloc_sbuf_tensor("DDIAG", [128, 128], F32)
    DOFF = nc.alloc_sbuf_tensor("DOFF", [128, 128], F32)
    TRI = nc.alloc_sbuf_tensor("TRI", [128, 128], BF16)
    OFFS = nc.alloc_sbuf_tensor("OFFS", [128, 16], F32)
    ONESRB = nc.alloc_sbuf_tensor("ONESRB", [1, 128], BF16)
    ONESCB = nc.alloc_sbuf_tensor("ONESCB", [128, 1], BF16)
    ONESRF = nc.alloc_sbuf_tensor("ONESRF", [1, 128], F32)
    ONESCF = nc.alloc_sbuf_tensor("ONESCF", [128, 1], F32)
    W0C = nc.alloc_sbuf_tensor("W0C", [128, 16], F32)
    BIASC = nc.alloc_sbuf_tensor("BIASC", [128, 16], F32)
    LNDV = nc.alloc_sbuf_tensor("LNDV", [128, 2], F32)
    LB0 = nc.alloc_sbuf_tensor("LB0", [128, 16], F32)
    LB1 = nc.alloc_sbuf_tensor("LB1", [128, 16], F32)
    GACC = nc.alloc_sbuf_tensor("GACC", [128, 4], F32)
    GACCT = nc.alloc_sbuf_tensor("GACCT", [128, 1], F32)
    GACCS = nc.alloc_sbuf_tensor("GACCS", [128, 132], F32)
    GDUM = nc.alloc_sbuf_tensor("GDUM", [128, 64], BF16)
    RG = nc.alloc_sbuf_tensor("RG", [128, 3], F32)
    DVS = nc.alloc_sbuf_tensor("DVS", [1, 8], F32)
    GROW = nc.alloc_sbuf_tensor("GROW", [1, 16], F32)
    W1B = nc.alloc_sbuf_tensor("W1B", [128, D], BF16)
    EPSC = nc.alloc_sbuf_tensor("EPSC", [128, 1], F32)
    assert nc.sbuf_bytes_remaining > 1024, nc.sbuf_bytes_remaining

    P0 = nc.alloc_psum_tensor("P0", [128, HT], F32)
    P1 = nc.alloc_psum_tensor("P1", [128, HT], F32)
    P2 = nc.alloc_psum_tensor("P2", [128, HT], F32)
    P3 = nc.alloc_psum_tensor("P3", [128, HT], F32)

    S = Sched()

    def d(eng, sem, out, in_, waits=()):
        return S.dma(eng, sem, (lambda e, out=out, in_=in_: e.dma_start(out=out, in_=in_)),
                     waits)

    _nold = [0]

    def mm_nold(e, out, lhsT, rhs, start, stop):
        with nc.discard():
            x = e.matmul(out, lhsT, rhs, start=start, stop=stop).ins
        _nold[0] += 1
        y = mybir.InstMatmult(
            name=f"mmnl{_nold[0]}", engine=x.engine, ins=list(x.ins),
            outs=list(x.outs),
            replication_resolution=x.replication_resolution,
            replication_shift_amnt=x.replication_shift_amnt,
            replication_num_rows=x.replication_num_rows,
            start_tensor_calc=x.start_tensor_calc,
            stop_tensor_calc=x.stop_tensor_calc,
            ldweights=False)
        return e.add_instruction(y)

    for out, in_ in ((DDIAG[:], c_ddiag), (DOFF[:], c_doff), (TRI[:], c_tri),
                     (OFFS[:], c_offs), (ONESRB[:], c_ones_r_bf), (ONESCB[:], c_ones_c_bf),
                     (ONESRF[:], c_ones_r_f), (ONESCF[:], c_ones_c_f), (W0C[:], w0c_d),
                     (BIASC[:], biasc_d), (DVS[0:1, 0:2], dv_d), (W1ROWF[:], w1row_d)):
        d("sync", "qs", out, in_)
    q_setup = S.cnt["qs"]

    def x_dma(h):
        in_ = x_fm[:, h * HT:(h + 1) * HT].rearrange("(c p) n -> p c n", p=128)
        out = XB[:].rearrange("p (c n) -> p c n", c=16)
        return S.dma("gpsimd", "qx", lambda e, o=out, i=in_: e.dma_start(out=o, in_=i))

    qg_x0 = x_dma(0)

    ev_z = S.emit("vector", lambda e: e.memset(GACCS[:], 0.0), inc="ev")
    ev_z = S.emit("vector", lambda e: e.memset(GACC[:], 0.0), waits=[("ev", ev_z)], inc="ev")
    ev_eps = S.emit("vector", lambda e: e.memset(EPSC[:], LN_EPS), inc="ev")
    gjobs = [(mat, sl) for sl in range(NCH) for mat in range(3)]
    gstate = {"next": 0, "cons": {}}
    gv_done = {}
    gsrcs = (w1t, w2t, w3t)

    def emit_gamma_jobs(n):
        while n > 0 and gstate["next"] < len(gjobs):
            idx = gstate["next"]
            gstate["next"] += 1
            n -= 1
            mat, sl = gjobs[idx]
            sem = f"qg{mat + 1}" + ("a" if sl % 2 == 0 else "b")
            w = []
            if (mat, sl - 2) in gstate["cons"]:
                w.append(gstate["cons"][(mat, sl - 2)])
            slot = mat * 2 + (sl % 2)
            seg = GRB[:, slot * 2048:(slot + 1) * 2048]
            cq = S.dma("gpsimd", sem,
                       lambda e, seg=seg, mat=mat, sl=sl: e.dma_start(
                           out=seg.rearrange("p (c n) -> p c n", c=16),
                           in_=gsrcs[mat][sl]),
                       waits=w)
            col = GACCS[:, mat * NCH + sl:mat * NCH + sl + 1]
            if sl % 2 == 1:
                cc = S.emit("vector",
                            lambda e, seg=seg, col=col: e.reduce_sum(
                                col, seg, axis=mybir.AxisListType.X,
                                apply_absolute_value=True),
                            waits=[(sem, cq), ("ev", ev_z)], inc="ev")
                gstate["cons"][(mat, sl)] = ("ev", cc)
            else:
                cc = S.emit("scalar",
                            lambda e, seg=seg, col=col: e.activation(
                                seg, seg, AF.Abs, accum_out=col),
                            waits=[(sem, cq), ("ev", ev_z)], inc="es")
                gstate["cons"][(mat, sl)] = ("es", cc)

    def finish_gamma_scan():
        emit_gamma_jobs(len(gjobs))
        p1 = S.emit("scalar", lambda e: e.copy(GACCT[:], GACCS[:, 0:1]), inc="es")
        p2 = S.emit("scalar", lambda e: e.copy(GACCT[:], GACCS[:, 1:2]),
                    waits=[("es", p1)], inc="es")
        for mat in range(3):
            last = [gstate["cons"][(mat, sl)] for sl in range(NCH)] + [("es", p2)]
            gv_done[mat] = S.emit(
                "vector",
                lambda e, mat=mat: e.reduce_sum(
                    GACC[:, mat:mat + 1], GACCS[:, mat * NCH:(mat + 1) * NCH],
                    axis=mybir.AxisListType.X),
                waits=last, inc="ev")

    ev_dv = S.emit("vector",
                   lambda e: e.tensor_scalar(DVS[0:1, 0:2], DVS[0:1, 0:2], 0.9, 1.0,
                                             ALU.max, ALU.min),
                   waits=[("qs", q_setup)], inc="ev")
    es_ln = S.emit("scalar",
                   lambda e: e.activation(DVS[0:1, 2:4], DVS[0:1, 0:2], AF.Ln),
                   waits=[("ev", ev_dv)], inc="es")
    ep_dvb = S.emit("tensor",
                    lambda e: e.matmul(P0[:, 0:2], ONESRF[:], DVS[0:1, 2:4],
                                       start=True, stop=True),
                    waits=[("es", es_ln)], inc="ep")
    es_dvc = S.emit("scalar", lambda e: e.copy(LNDV[:], P0[:, 0:2]),
                    waits=[("ep", ep_dvb)], inc="es")
    ev_lb = None
    for i, LB in enumerate((LB0, LB1)):
        ev_lb = S.emit("vector",
                       lambda e, LB=LB, i=i: e.tensor_scalar(LB[:], OFFS[:],
                                                             LNDV[:, i:i + 1], None, ALU.mult),
                       waits=[("es", es_dvc)], inc="ev")

    def setup_half(h):
        if h == 1:
            qW = d("sync", "qs", W1ROWF[:], w1row_d)
            w = [("qs", qW)]
        else:
            w = [("qs", q_setup)]
        es_wr = S.emit("scalar", lambda e: e.copy(W1ROWBF[:], W1ROWF[:]),
                       waits=w, inc="es")
        ep_w = None
        for half in range(2):
            P = (P0, P1)[half]
            for s in range(2):
                first = (half == 0 and s == 0)
                ep_w = S.emit("tensor",
                              lambda e, P=P, half=half, s=s, first=first: (
                                  e.matmul if first else (
                                      lambda o, l, r, start, stop: mm_nold(
                                          e, o, l, r, start, stop)))(
                                  P[:, s * 512:(s + 1) * 512], ONESRB[:],
                                  W1ROWBF[0:1, half * 1024 + s * 512:half * 1024 + (s + 1) * 512],
                                  start=True, stop=True),
                              waits=[("es", es_wr)], inc="ep")
        es_wb = None
        for half in range(2):
            P = (P0, P1)[half]
            es_wb = S.emit("scalar",
                           lambda e, P=P, half=half: e.copy(
                               W1B[:, half * 1024:(half + 1) * 1024], P[:]),
                           waits=[("ep", ep_w)], inc="es")
        es_exp = None
        for m in range(16):
            DDm = DDIAG if m == 0 else DOFF
            es_exp = S.emit("scalar",
                            lambda e, m=m, DDm=DDm: e.activation(
                                EXPA[:, m * 128:(m + 1) * 128], DDm[:], AF.Exp,
                                bias=LB0[:, m:m + 1], scale=LNDV[:, 0:1]),
                            waits=[("ev", ev_lb), ("es", es_wb)], inc="es")
            es_exp = S.emit("scalar",
                            lambda e, m=m, DDm=DDm: e.activation(
                                EXPC[:, m * 128:(m + 1) * 128], DDm[:], AF.Exp,
                                bias=LB1[:, m:m + 1], scale=LNDV[:, 1:2]),
                            inc="es")
        return es_exp, es_wb

    def phase_A(h, es_setup, es_wb, qg_x):
        tile_build = {}
        tile_done = {}
        evac = {}
        stats_done = {}
        gctr = [0]

        def mixer_j(j):
            g = gctr[0]
            if USE_BCAST_AP:
                n = j + 1
                in0 = EXPC[:, 0:n * 128].rearrange("p (m q) -> p m q", m=n)
                in1 = W1B[:, j * 128:(j + 1) * 128].rearrange(
                    "p (o q) -> p o q", o=1).broadcast_to([128, n, 128])
                o3 = C2ROW[:, 0:n * 128].rearrange("p (m q) -> p m q", m=n)
                S.emit("vector",
                       lambda e, o3=o3, in0=in0, in1=in1: e.tensor_tensor(o3, in0, in1,
                                                                          ALU.mult),
                       waits=[("es", es_setup)])
            else:
                for m in range(j + 1):
                    S.emit("vector",
                           lambda e, m=m, j=j: e.tensor_tensor(
                               C2ROW[:, m * 128:(m + 1) * 128],
                               EXPC[:, m * 128:(m + 1) * 128],
                               W1B[:, j * 128:(j + 1) * 128], ALU.mult),
                           waits=[("es", es_setup)])
            P = (P0, P1)[j % 2]
            for k in range(j + 1):
                m = j - k
                slot = g % 17
                w = [("es", es_setup)]
                if g - 17 in tile_done:
                    w.append(("ep", tile_done[g - 17]))
                cv = S.emit("vector",
                            lambda e, m=m, k=k, slot=slot: e.scalar_tensor_tensor(
                                MRING[:, slot * 128:(slot + 1) * 128],
                                EXPA[:, m * 128:(m + 1) * 128], W0C[:, k:k + 1],
                                C2ROW[:, m * 128:(m + 1) * 128], ALU.mult, ALU.add),
                            waits=w, inc="ev")
                if k == j:
                    cv = S.emit("vector",
                                lambda e, slot=slot: e.tensor_tensor(
                                    MRING[:, slot * 128:(slot + 1) * 128],
                                    MRING[:, slot * 128:(slot + 1) * 128], TRI[:], ALU.mult),
                                inc="ev")
                tile_build[g] = cv
                g += 1
            for k in range(j + 1):
                gk = g - (j + 1) + k
                slot = gk % 17
                w = []
                if k == 0:
                    w = [("ev", tile_build[g - 1]), ("qx", qg_x), ("es", es_wb)]
                    if (j - 2) in evac:
                        w.append(("ev", evac[j - 2]))
                S.emit("tensor",
                       lambda e, P=P, slot=slot, k=k, j=j: e.matmul(
                           P[:, 0:512], MRING[:, slot * 128:(slot + 1) * 128],
                           XB[:, k * HT:k * HT + 512], start=(k == 0), stop=(k == j)),
                       waits=w)
                cp = S.emit("tensor",
                            lambda e, P=P, slot=slot, k=k, j=j: mm_nold(
                                e, P[:, 512:1024], MRING[:, slot * 128:(slot + 1) * 128],
                                XB[:, k * HT + 512:(k + 1) * HT], (k == 0), (k == j)),
                            inc="ep")
                tile_done[gk] = cp
            gctr[0] = g

        def post_j(j):
            P = (P0, P1)[j % 2]
            cv = S.emit("vector",
                        lambda e, P=P, j=j: e.scalar_tensor_tensor(
                            H1[:, j * HT:(j + 1) * HT], P[:], BIASC[:, j:j + 1],
                            XB[:, j * HT:(j + 1) * HT], ALU.add, ALU.add),
                        waits=[("ep", tile_done[(j + 1) * (j + 2) // 2 - 1]),
                               ("qx", qg_x)], inc="ev")
            evac[j] = cv
            w = [("ev", cv)]
            if j - 1 in stats_done:
                w.append(("ep", stats_done[j - 1]))
            cs = S.emit("scalar",
                        lambda e, j=j: e.activation(SQ[:], H1[:, j * HT:(j + 1) * HT],
                                                    AF.Square),
                        waits=w, inc="es")
            cp2 = None
            for s in range(2):
                S.emit("tensor",
                       lambda e, j=j, s=s: (e.matmul(
                           P2[0:1, s * 512:(s + 1) * 512], ONESCB[:],
                           H1[:, j * HT + s * 512:j * HT + (s + 1) * 512],
                           start=(j == 0), stop=(j == 15)) if s == 0 else mm_nold(
                           e, P2[0:1, s * 512:(s + 1) * 512], ONESCB[:],
                           H1[:, j * HT + s * 512:j * HT + (s + 1) * 512],
                           (j == 0), (j == 15))),
                       waits=[("ev", cv)])
                cp2 = S.emit("tensor",
                             lambda e, j=j, s=s: mm_nold(
                                 e, P3[0:1, s * 512:(s + 1) * 512], ONESCB[:],
                                 SQ[:, s * 512:(s + 1) * 512],
                                 (j == 0), (j == 15)),
                             waits=[("es", cs)], inc="ep")
            stats_done[j] = cp2

        for j in range(16):
            mixer_j(j)
            if j >= 1:
                post_j(j - 1)
            if h == 0:
                emit_gamma_jobs(9)
        post_j(15)
        return stats_done[15], evac

    def ln_finalize(stats_ep, last_z_ev):
        c1 = S.emit("vector",
                    lambda e: e.tensor_scalar(MROWF[:], P2[0:1, :], 1.0 / D, 0.0,
                                              ALU.mult, ALU.add),
                    waits=[("ep", stats_ep), ("ev", last_z_ev)], inc="ev")
        S.emit("vector",
               lambda e: e.tensor_tensor(VROW[:], MROWF[:], MROWF[:], ALU.mult))
        c3 = S.emit("vector",
                    lambda e: e.scalar_tensor_tensor(VROW[:], P3[0:1, :], 1.0 / D,
                                                     VROW[:], ALU.mult, ALU.subtract),
                    inc="ev")
        S.emit("scalar",
               lambda e: e.activation(VROW[:], VROW[:], AF.Ln, bias=EPSC[0:1, 0:1]),
               waits=[("ev", c3), ("ev", ev_eps)])
        c5 = S.emit("scalar",
                    lambda e: e.activation(RROW[:], VROW[:], AF.Exp, scale=-0.5),
                    inc="es")
        c6 = S.emit("vector",
                    lambda e: e.scalar_tensor_tensor(BROW[:], MROWF[:], -1.0, RROW[:],
                                                     ALU.mult, ALU.mult),
                    waits=[("es", c5)], inc="ev")
        cp = None
        for s in range(2):
            S.emit("tensor",
                   lambda e, s=s: (e.matmul(P0[:, s * 512:(s + 1) * 512], ONESRB[:],
                                            RROW[0:1, s * 512:(s + 1) * 512],
                                            start=True, stop=True) if s == 0 else
                                   mm_nold(e, P0[:, s * 512:(s + 1) * 512], ONESRB[:],
                                           RROW[0:1, s * 512:(s + 1) * 512], True, True)),
                   waits=[("ev", c6)])
            cp = S.emit("tensor",
                        lambda e, s=s: mm_nold(
                            e, P1[:, s * 512:(s + 1) * 512], ONESRB[:],
                            BROW[0:1, s * 512:(s + 1) * 512], True, True),
                        inc="ep")
        S.emit("scalar", lambda e: e.copy(RB[:], P0[:]), waits=[("ep", cp)])
        c8 = S.emit("scalar", lambda e: e.copy(BB[:], P1[:]), inc="es")
        return c8

    def gamma_finalize():
        w = [("ev", gv_done[m]) for m in range(3)]
        cp = S.emit("tensor",
                    lambda e: e.matmul(P2[0:1, 0:3], ONESCF[:], GACC[:, 0:3],
                                       start=True, stop=True),
                    waits=w, inc="ep")
        c1 = S.emit("vector",
                    lambda e: e.tensor_scalar(GROW[0:1, 0:3], P2[0:1, 0:3], 1.0 / NW,
                                              G_EPS, ALU.mult, ALU.add),
                    waits=[("ep", cp)], inc="ev")
        c2 = S.emit("vector",
                    lambda e: e.reciprocal(GROW[0:1, 4:7], GROW[0:1, 0:3]),
                    waits=[("ev", c1)], inc="ev")
        cp2 = S.emit("tensor",
                     lambda e: e.matmul(P2[:, 0:3], ONESRF[:], GROW[0:1, 4:7],
                                        start=True, stop=True),
                     waits=[("ev", c2)], inc="ep")
        c3 = S.emit("scalar", lambda e: e.copy(RG[:], P2[:, 0:3]),
                    waits=[("ep", cp2)], inc="es")
        return c3

    def normalize_bf16(ln_es):
        last = None
        for j in range(16):
            S.emit("vector",
                   lambda e, j=j: e.tensor_tensor(
                       H1[:, j * HT:(j + 1) * HT], H1[:, j * HT:(j + 1) * HT],
                       RB[:], ALU.mult),
                   waits=[("es", ln_es)])
            last = S.emit("vector",
                          lambda e, j=j: e.tensor_tensor(
                              H1[:, j * HT:(j + 1) * HT], H1[:, j * HT:(j + 1) * HT],
                              BB[:], ALU.add),
                          inc="ev")
        return last

    def normalize_store(ln_es, h):
        stage_dma = {}
        for j in range(16):
            S.emit("vector",
                   lambda e, j=j: e.tensor_tensor(
                       TNORM[:], H1[:, j * HT:(j + 1) * HT], RB[:], ALU.mult),
                   waits=[("es", ln_es)])
            for s in range(2):
                u = j * 2 + s
                STG = (STAGEA, STAGEB)[u % 2]
                par = "qST" + ("a" if u % 2 == 0 else "b")
                w = []
                if u - 2 in stage_dma:
                    w.append(stage_dma[u - 2])
                cv = S.emit("vector",
                            lambda e, s=s, STG=STG: e.tensor_tensor(
                                STG[:], TNORM[:, s * 512:(s + 1) * 512],
                                BB[:, s * 512:(s + 1) * 512], ALU.add),
                            waits=w, inc="ev")
                stage_dma[u] = (par, S.dma(
                    "sync", par,
                    lambda e, j=j, s=s, STG=STG: e.dma_start(
                        out=out_fm[j * 128:(j + 1) * 128,
                                   h * HT + s * 512:h * HT + (s + 1) * 512],
                        in_=STG[:]),
                    waits=[("ev", cv)]))

    def phase_B(h, rg_es, norm_ev):
        mult_done = {}
        sig_done = {}
        pe_done = {}
        gp_done = {}
        q_ready_d = {}
        QMAT = {0: (L1, Q1R, w1t, q1c, "sync", "qL1", "qc1"),
                1: (L2, Q2R, w2t, q2c, "scalar", "qL2", "qc2"),
                2: (L3, Q3S, w3t, q3c, "gpsimd", "qL3", "qc3")}

        def quant(mat, hb):
            L, QOUT, src_t, qcache, eng_q, semL, semC = QMAT[mat]
            hs = hb % 2
            for piece in range(2):
                u = 2 * hb + piece
                slot = u % 2
                sem = semL + ("a" if slot == 0 else "b")
                seg = L[:, slot * 1024:(slot + 1) * 1024]
                w = []
                if (mat, u - 2) in gp_done:
                    w.append(("ev", gp_done[(mat, u - 2)]))
                cq = S.dma(eng_q, sem,
                           lambda e, seg=seg, src=src_t[hb], piece=piece: e.dma_start(
                               out=seg.rearrange("p (c n) -> p c n", c=8),
                               in_=src[:, piece * 8:(piece + 1) * 8, :]),
                           waits=w)
                cs = S.emit("scalar",
                            lambda e, seg=seg, mat=mat: e.activation(
                                seg, seg, AF.Copy, bias=MAGIC, scale=RG[:, mat:mat + 1]),
                            waits=[(sem, cq), ("es", rg_es)], inc="es")
                S.emit("vector",
                       lambda e, seg=seg: e.tensor_scalar(seg, seg, CM1, CP1,
                                                          ALU.max, ALU.min),
                       waits=[("es", cs)])
                w = [(semC, S.cnt[semC])]
                if hb - 2 in pe_done and mat < 2:
                    w.append(("ep", pe_done[hb - 2]))
                qcol = hs * D + piece * 1024
                cg = S.emit("vector",
                            lambda e, seg=seg, QOUT=QOUT, qcol=qcol: e.tensor_scalar(
                                QOUT[:, qcol:qcol + 1024], seg, MAGIC, None, ALU.subtract),
                            waits=w, inc="ev")
                gp_done[(mat, u)] = cg
            co = (q3c[hb].rearrange("p c n -> p (c n)") if mat == 2 else qcache[hb])
            S.dma(eng_q, semC,
                  lambda e, co=co, QOUT=QOUT, hs=hs: e.dma_start(
                      out=co, in_=QOUT[:, hs * D:(hs + 1) * D]),
                  waits=[("ev", gp_done[(mat, 2 * hb + 1)])])

        def produce(hb):
            hs = hb % 2
            if h == 0:
                for mat in range(3):
                    quant(mat, hb)
            else:
                w = []
                if hb - 2 in pe_done:
                    w.append(("ep", pe_done[hb - 2]))
                par = "a" if hs == 0 else "b"
                c1 = S.dma("sync", "qR1" + par,
                           lambda e, hs=hs, hb=hb: e.dma_start(
                               out=Q1R[:, hs * D:(hs + 1) * D], in_=q1c[hb]), w)
                c2 = S.dma("scalar", "qR2" + par,
                           lambda e, hs=hs, hb=hb: e.dma_start(
                               out=Q2R[:, hs * D:(hs + 1) * D], in_=q2c[hb]), w)
                q_ready_d[hb] = [("qR1" + par, c1), ("qR2" + par, c2)]

        def consume(hb):
            hs = hb % 2
            if h == 0:
                q_ready = [("ev", gp_done[(0, 2 * hb + 1)]),
                           ("ev", gp_done[(1, 2 * hb + 1)])]
            else:
                q_ready = q_ready_d[hb]
            PA, PB = (P0, P1) if hs == 0 else (P2, P3)
            w = list(q_ready) + [("ev", norm_ev)]
            if hb - 2 in mult_done:
                w.append(("ev", mult_done[hb - 2]))
            cp = None
            for QR, PX in ((Q1R, PA), (Q2R, PB)):
                for dc in range(NCD):
                    for s in range(2):
                        last = (QR is Q2R and dc == NCD - 1 and s == 1)
                        cp = S.emit("tensor",
                                    lambda e, QR=QR, PX=PX, hs=hs, dc=dc, s=s: (e.matmul(
                                        PX[:, s * 512:(s + 1) * 512],
                                        QR[:, hs * D + dc * 128:hs * D + (dc + 1) * 128],
                                        H1[:, dc * HT + s * 512:dc * HT + (s + 1) * 512],
                                        start=(dc == 0), stop=(dc == NCD - 1)) if s == 0
                                        else mm_nold(
                                        e, PX[:, s * 512:(s + 1) * 512],
                                        QR[:, hs * D + dc * 128:hs * D + (dc + 1) * 128],
                                        H1[:, dc * HT + s * 512:dc * HT + (s + 1) * 512],
                                        (dc == 0), (dc == NCD - 1))),
                                    waits=(w if QR is Q1R and dc == 0 and s == 0 else ()),
                                    inc=("ep" if last else None))
            pe_done[hb] = cp
            w = [("ep", cp)]
            if hb - 1 in mult_done:
                w.append(("ev", mult_done[hb - 1]))
            cs = S.emit("scalar",
                        lambda e, PA=PA: e.activation(SIG[:], PA[:], AF.Sigmoid),
                        waits=w, inc="es")
            sig_done[hb] = cs
            cv = S.emit("vector",
                        lambda e, PB=PB, hb=hb: e.tensor_tensor(
                            HID[:, hb * HT:(hb + 1) * HT], PB[:], SIG[:], ALU.mult),
                        waits=[("es", cs)], inc="ev")
            mult_done[hb] = cv

        for hb in range(NCH):
            produce(hb)
            if hb >= 1:
                consume(hb - 1)
        consume(NCH - 1)
        return mult_done[NCH - 1]

    def phase_C(h, hid_ev):
        evac = {}
        stats_done = {}
        pe_f = {}

        col_cnt = {}

        def col_dma(dco):
            slot = dco % 2
            w = []
            if dco - 2 in pe_f:
                w.append(("ep", pe_f[dco - 2]))
            col_cnt[dco] = S.dma(
                "sync", "qCC" + ("a" if slot == 0 else "b"),
                lambda e, slot=slot, dco=dco: e.dma_start(
                    out=QC[:, slot * 5632:(slot + 1) * 5632].rearrange(
                        "p (a b) -> p a b", a=NCH),
                    in_=q3c[:, :, dco, :].rearrange("a p b -> p a b")),
                waits=w)

        def cpe(dco):
            slot = dco % 2
            P = (P0, P1)[slot]
            w = [("qCC" + ("a" if slot == 0 else "b"), col_cnt[dco]), ("ev", hid_ev)]
            if dco - 2 in evac:
                w.append(("ev", evac[dco - 2]))
            cp = None
            for k in range(NCH):
                for s in range(2):
                    last = (k == NCH - 1 and s == 1)
                    cp = S.emit("tensor",
                                lambda e, P=P, slot=slot, k=k, s=s: (e.matmul(
                                    P[:, s * 512:(s + 1) * 512],
                                    QC[:, slot * 5632 + k * 128:slot * 5632 + (k + 1) * 128],
                                    HID[:, k * HT + s * 512:k * HT + (s + 1) * 512],
                                    start=(k == 0), stop=(k == NCH - 1)) if s == 0
                                    else mm_nold(
                                    e, P[:, s * 512:(s + 1) * 512],
                                    QC[:, slot * 5632 + k * 128:slot * 5632 + (k + 1) * 128],
                                    HID[:, k * HT + s * 512:k * HT + (s + 1) * 512],
                                    (k == 0), (k == NCH - 1))),
                                waits=(w if k == 0 and s == 0 else ()),
                                inc=("ep" if last else None))
            pe_f[dco] = cp

        def cpost(dco):
            P = (P0, P1)[dco % 2]
            cv = S.emit("vector",
                        lambda e, P=P, dco=dco: e.tensor_tensor(
                            H1[:, dco * HT:(dco + 1) * HT], P[:],
                            H1[:, dco * HT:(dco + 1) * HT], ALU.add),
                        waits=[("ep", pe_f[dco])], inc="ev")
            evac[dco] = cv
            w = [("ev", cv)]
            if dco - 1 in stats_done:
                w.append(("ep", stats_done[dco - 1]))
            cs = S.emit("scalar",
                        lambda e, dco=dco: e.activation(SQ[:], H1[:, dco * HT:(dco + 1) * HT],
                                                        AF.Square),
                        waits=w, inc="es")
            cp2 = None
            for s in range(2):
                S.emit("tensor",
                       lambda e, dco=dco, s=s: (e.matmul(
                           P2[0:1, s * 512:(s + 1) * 512], ONESCB[:],
                           H1[:, dco * HT + s * 512:dco * HT + (s + 1) * 512],
                           start=(dco == 0), stop=(dco == NCD - 1)) if s == 0 else
                           mm_nold(
                           e, P2[0:1, s * 512:(s + 1) * 512], ONESCB[:],
                           H1[:, dco * HT + s * 512:dco * HT + (s + 1) * 512],
                           (dco == 0), (dco == NCD - 1))),
                       waits=[("ev", cv)])
                cp2 = S.emit("tensor",
                             lambda e, dco=dco, s=s: mm_nold(
                                 e, P3[0:1, s * 512:(s + 1) * 512], ONESCB[:],
                                 SQ[:, s * 512:(s + 1) * 512],
                                 (dco == 0), (dco == NCD - 1)),
                             waits=[("es", cs)], inc="ep")
            stats_done[dco] = cp2

        for dco in range(NCD):
            col_dma(dco)
            if dco >= 1:
                cpe(dco - 1)
            if dco >= 2:
                cpost(dco - 2)
        cpe(NCD - 1)
        cpost(NCD - 2)
        cpost(NCD - 1)
        ln_es = ln_finalize(stats_done[NCD - 1], evac[NCD - 1])
        normalize_store(ln_es, h)

    es_setup, es_wb = setup_half(0)
    stats_ep, evacA = phase_A(0, es_setup, es_wb, qg_x0)
    ln_es = ln_finalize(stats_ep, evacA[15])
    norm_ev = normalize_bf16(ln_es)
    if dump_h1:
        d("sync", "qs", dbg_h1[0], H1[:], waits=[("ev", norm_ev)])
    if phases >= 2:
        finish_gamma_scan()
        rg_es = gamma_finalize()
        S.barrier()
        hid_ev = phase_B(0, rg_es, norm_ev)
        if dump_hid:
            d("sync", "qs", dbg_hid[0], HID[:], waits=[("ev", hid_ev)])
        if dump_q:
            S.barrier()
            d("sync", "qs", dbg_q1[:], q1c)
            d("sync", "qs", dbg_q2[:], q2c)
            d("sync", "qs", dbg_rg, RG[:])
            d("sync", "qs", dbg_ga, GACC[:])
    if phases >= 3:
        S.barrier()
        phase_C(0, hid_ev)
        S.barrier()
        qg_x1 = x_dma(1)
        es_setup2, es_wb2 = setup_half(1)
        stats_ep2, evacA2 = phase_A(1, es_setup2, es_wb2, qg_x1)
        ln_es2 = ln_finalize(stats_ep2, evacA2[15])
        norm_ev2 = normalize_bf16(ln_es2)
        if dump_h1:
            d("sync", "qs", dbg_h1[1], H1[:], waits=[("ev", norm_ev2)])
        S.barrier()
        hid_ev2 = phase_B(1, rg_es, norm_ev2)
        if dump_hid:
            d("sync", "qs", dbg_hid[1], HID[:], waits=[("ev", hid_ev2)])
        S.barrier()
        phase_C(1, hid_ev2)

    qfin = [(n, S.cnt[n]) for n in S.cnt if n.startswith("q")]
    S.emit("sync", lambda e: e.nop(), waits=qfin)

    with contextlib.ExitStack() as stack:
        sems = {n: stack.enter_context(nc.semaphore(n)) for n in S.cnt}
        block = stack.enter_context(nc.Block())

        def make_body(eng_name):
            def body(e):
                for waits, fn, inc, amt in S.ops[eng_name]:
                    for s, v in waits:
                        e.wait_ge(sems[s], v)
                    ins = fn(e)
                    if inc is not None:
                        ins.then_inc(sems[inc], amt)
            return body

        for eng_name in ENGS:
            getattr(block, eng_name)(make_body(eng_name))
    return nc


def _host_layout(L0_attn_x, L0_attn_weight, L0_attn_bias, L0_attn_decay_value,
                 L0_ffn_w1, L0_ffn_w2, L0_ffn_w3):
    f32 = np.float32
    x2 = np.ascontiguousarray(np.asarray(L0_attn_x, f32).reshape(8 * TOK, D))
    w1t = np.ascontiguousarray(np.asarray(L0_ffn_w1, f32).reshape(NCH, 128, NCD, 128)
                               .transpose(0, 3, 2, 1))
    w2t = np.ascontiguousarray(np.asarray(L0_ffn_w2, f32).reshape(NCH, 128, NCD, 128)
                               .transpose(0, 3, 2, 1))
    w3t = np.ascontiguousarray(np.asarray(L0_ffn_w3, f32).reshape(NCD, 128, NCH, 128)
                               .transpose(2, 3, 0, 1))
    wt = np.asarray(L0_attn_weight, f32)
    w0c = np.ascontiguousarray(wt[0].reshape(NCD, 128).T)
    w1row = np.ascontiguousarray(wt[1].reshape(1, D))
    biasc = np.ascontiguousarray(np.asarray(L0_attn_bias, f32).reshape(NCD, 128).T)
    dv = np.ascontiguousarray(np.asarray(L0_attn_decay_value, f32).reshape(1, 2))

    ii = np.arange(128, dtype=f32)
    dmat = ii[None, :] - ii[:, None]
    shared = {
        "c_ddiag": np.ascontiguousarray(np.maximum(dmat, 0.0).astype(f32)),
        "c_doff": np.ascontiguousarray(dmat.astype(f32)),
        "c_tri": np.ascontiguousarray((dmat >= 0).astype(ml_dtypes.bfloat16)),
        "c_offs": np.ascontiguousarray(
            np.broadcast_to(128.0 * np.arange(16, dtype=f32), (128, 16))),
        "c_ones_r_bf": np.ones((1, 128), ml_dtypes.bfloat16),
        "c_ones_c_bf": np.ones((128, 1), ml_dtypes.bfloat16),
        "c_ones_r_f": np.ones((1, 128), f32),
        "c_ones_c_f": np.ones((128, 1), f32),
        "w1t": w1t, "w2t": w2t, "w3t": w3t, "w0c": w0c, "biasc": biasc,
        "dv": dv, "w1row": w1row,
    }
    in_maps = []
    for c in range(8):
        m = dict(shared)
        m["x_fm"] = np.ascontiguousarray(x2[c * TOK:(c + 1) * TOK, :].T)
        in_maps.append(m)
    return in_maps


def kernel(L0_attn_x, L0_attn_weight, L0_attn_bias, L0_attn_decay_value,
           L0_ffn_w1, L0_ffn_w2, L0_ffn_w3):
    in_maps = _host_layout(L0_attn_x, L0_attn_weight, L0_attn_bias,
                           L0_attn_decay_value, L0_ffn_w1, L0_ffn_w2, L0_ffn_w3)
    nc = build_program()
    res = run_bass_kernel_spmd(nc, in_maps, core_ids=list(range(8)))
    parts = [np.asarray(r["out_fm"]).T for r in res.results]
    return np.ascontiguousarray(np.concatenate(parts, axis=0)
                                .reshape(2, 8192, D)).astype(np.float32)


# version 17
# speedup vs baseline: 1.4874x; 1.0078x over previous
import contextlib
import numpy as np
import ml_dtypes
import concourse.bass as bass
import concourse.mybir as mybir
from concourse.bass_utils import run_bass_kernel_spmd

F32 = mybir.dt.float32
BF16 = mybir.dt.bfloat16
AF = mybir.ActivationFunctionType
ALU = mybir.AluOpType

MAGIC = float(np.float32(1.5 * 2 ** 23))
CM1 = float(np.float32(MAGIC - 1.0))
CP1 = float(np.float32(MAGIC + 1.0))
LN_EPS = 1e-5
G_EPS = 1e-8
NW = 5632 * 2048
D = 2048
H = 5632
NCD = 16
NCH = 44
TOK = 2048
HT = 1024
ENGS = ("sync", "gpsimd", "vector", "scalar", "tensor")

USE_BCAST_AP = True


class Sched:

    def __init__(self):
        self.ops = {e: [] for e in ENGS}
        names = ["qs", "qa", "qg", "qx", "qc1", "qc2", "qc3",
                 "ep", "ev", "es", "eg"]
        for b in ("qg1", "qg2", "qg3", "qL1", "qL2", "qL3", "qR1", "qR2",
                  "qCC", "qST"):
            names += [b + "a", b + "b"]
        self.cnt = {n: 0 for n in names}
        self.hw = {e: {} for e in ENGS}

    def emit(self, eng, fn, waits=(), inc=None, amt=1):
        w = []
        for s, v in waits:
            if v is None or v <= 0:
                continue
            assert v <= self.cnt[s], f"wait {s}>={v} but only {self.cnt[s]} scheduled"
            if self.hw[eng].get(s, 0) >= v:
                continue
            self.hw[eng][s] = v
            w.append((s, v))
        val = None
        if inc is not None:
            self.cnt[inc] += amt
            val = self.cnt[inc]
        self.ops[eng].append((tuple(w), fn, inc, amt))
        return val

    def dma(self, eng, sem, fn, waits=()):
        return self.emit(eng, fn, waits, inc=sem, amt=16)

    def barrier(self):
        snap = [(s, v) for s, v in self.cnt.items()]
        for eng, sem in (("tensor", "ep"), ("vector", "ev"),
                         ("scalar", "es"), ("gpsimd", "eg")):
            self.emit(eng, lambda e: e.nop(), waits=snap, inc=sem)
        snap2 = [(s, v) for s, v in self.cnt.items()]
        for eng in ENGS:
            self.emit(eng, lambda e: e.nop(), waits=snap2)


def build_program(phases=3, dump_h1=False, dump_hid=False, dump_q=False):
    nc = bass.Bass("TRN2", target_bir_lowering=False)

    x_fm = nc.dram_tensor("x_fm", [D, TOK], F32, kind="ExternalInput").ap()
    w1t = nc.dram_tensor("w1t", [NCH, 128, NCD, 128], F32, kind="ExternalInput").ap()
    w2t = nc.dram_tensor("w2t", [NCH, 128, NCD, 128], F32, kind="ExternalInput").ap()
    w3t = nc.dram_tensor("w3t", [NCH, 128, NCD, 128], F32, kind="ExternalInput").ap()
    c_ddiag = nc.dram_tensor("c_ddiag", [128, 128], F32, kind="ExternalInput").ap()
    c_doff = nc.dram_tensor("c_doff", [128, 128], F32, kind="ExternalInput").ap()
    c_tri = nc.dram_tensor("c_tri", [128, 128], BF16, kind="ExternalInput").ap()
    c_offs = nc.dram_tensor("c_offs", [128, 16], F32, kind="ExternalInput").ap()
    c_ones_r_bf = nc.dram_tensor("c_ones_r_bf", [1, 128], BF16, kind="ExternalInput").ap()
    c_ones_c_bf = nc.dram_tensor("c_ones_c_bf", [128, 1], BF16, kind="ExternalInput").ap()
    c_ones_r_f = nc.dram_tensor("c_ones_r_f", [1, 128], F32, kind="ExternalInput").ap()
    c_ones_c_f = nc.dram_tensor("c_ones_c_f", [128, 1], F32, kind="ExternalInput").ap()
    w0c_d = nc.dram_tensor("w0c", [128, 16], F32, kind="ExternalInput").ap()
    biasc_d = nc.dram_tensor("biasc", [128, 16], F32, kind="ExternalInput").ap()
    dv_d = nc.dram_tensor("dv", [1, 2], F32, kind="ExternalInput").ap()
    w1row_d = nc.dram_tensor("w1row", [1, D], F32, kind="ExternalInput").ap()

    out_fm = nc.dram_tensor("out_fm", [D, TOK], F32, kind="ExternalOutput").ap()

    q1c = nc.dram_tensor("q1c", [NCH, 128, D], BF16, kind="Internal").ap()
    q2c = nc.dram_tensor("q2c", [NCH, 128, D], BF16, kind="Internal").ap()
    q3c = nc.dram_tensor("q3c", [NCH, 128, NCD, 128], BF16, kind="Internal").ap()

    dbg_h1 = dbg_hid = dbg_q1 = dbg_q2 = dbg_rg = dbg_ga = None
    if dump_h1:
        dbg_h1 = nc.dram_tensor("dbg_h1", [2, 128, 16 * HT], BF16, kind="ExternalOutput").ap()
    if dump_hid:
        dbg_hid = nc.dram_tensor("dbg_hid", [2, 128, NCH * HT], BF16, kind="ExternalOutput").ap()
    if dump_q:
        dbg_q1 = nc.dram_tensor("dbg_q1", [NCH, 128, D], BF16, kind="ExternalOutput").ap()
        dbg_q2 = nc.dram_tensor("dbg_q2", [NCH, 128, D], BF16, kind="ExternalOutput").ap()
        dbg_rg = nc.dram_tensor("dbg_rg", [128, 3], F32, kind="ExternalOutput").ap()
        dbg_ga = nc.dram_tensor("dbg_ga", [128, 4], F32, kind="ExternalOutput").ap()

    REG = 32768 + 32768 + 90112 + 24576 + 20480
    base, _ = nc.bump_sbuf(REG)
    off_x, off_h1, off_hid, off_qr, off_scr = (
        base, base + 32768, base + 65536, base + 155648, base + 180224)

    def at(name, shape, dtype, off):
        return nc.alloc_sbuf_tensor_at(name, shape, dtype, offset=off)

    XB = at("XB", [128, 16 * HT], BF16, off_x)
    L1 = at("L1", [128, 2048], F32, off_x)
    L2 = at("L2", [128, 2048], F32, off_x + 8192)
    L3 = at("L3", [128, 2048], F32, off_x + 16384)
    H1 = at("H1", [128, 16 * HT], BF16, off_h1)
    HID = at("HID", [128, NCH * HT], BF16, off_hid)
    GR = at("GR", [128, 6 * 2048], F32, off_hid)
    GRB = at("GRB", [128, 6 * 2048], BF16, off_hid)
    Q1R = at("Q1R", [128, 2 * D], BF16, off_qr)
    Q2R = at("Q2R", [128, 2 * D], BF16, off_qr + 8192)
    Q3S = at("Q3S", [128, 2 * D], BF16, off_qr + 16384)
    QC = at("QC", [128, 2 * 5632], BF16, off_qr)
    W1ROWF = at("W1ROWF", [1, D], F32, off_qr)
    EXPA = at("EXPA", [128, 2048], BF16, off_scr)
    EXPC = at("EXPC", [128, 2048], BF16, off_scr + 4096)
    C2ROW = at("C2ROW", [128, 2048], BF16, off_scr + 8192)
    VROW = at("VROW", [1, HT], F32, off_scr)
    MROWF = at("MROWF", [1, HT], F32, off_scr + 4096)
    RROW = at("RROW", [1, HT], BF16, off_scr + 8192)
    BROW = at("BROW", [1, HT], BF16, off_scr + 10240)
    STAGEB = at("STAGEB", [128, 512], F32, off_scr)
    RB = at("RB", [128, HT], BF16, off_scr + 12288)
    SIG = at("SIG", [128, HT], BF16, off_scr + 12288)
    BB = at("BB", [128, HT], BF16, off_scr + 14336)
    SQ = at("SQ", [128, HT], BF16, off_scr + 16384)
    TNORM = at("TNORM", [128, HT], BF16, off_scr + 16384)
    MRING = at("MRING", [128, 17 * 128], BF16, off_qr + 8192)
    STAGEA = at("STAGEA", [128, 512], F32, off_scr + 18432)
    W1ROWBF = at("W1ROWBF", [1, D], BF16, off_scr + 8192)

    DDIAG = nc.alloc_sbuf_tensor("DDIAG", [128, 128], F32)
    DOFF = nc.alloc_sbuf_tensor("DOFF", [128, 128], F32)
    TRI = nc.alloc_sbuf_tensor("TRI", [128, 128], BF16)
    OFFS = nc.alloc_sbuf_tensor("OFFS", [128, 16], F32)
    ONESRB = nc.alloc_sbuf_tensor("ONESRB", [1, 128], BF16)
    ONESCB = nc.alloc_sbuf_tensor("ONESCB", [128, 1], BF16)
    ONESRF = nc.alloc_sbuf_tensor("ONESRF", [1, 128], F32)
    ONESCF = nc.alloc_sbuf_tensor("ONESCF", [128, 1], F32)
    W0C = nc.alloc_sbuf_tensor("W0C", [128, 16], F32)
    BIASC = nc.alloc_sbuf_tensor("BIASC", [128, 16], F32)
    LNDV = nc.alloc_sbuf_tensor("LNDV", [128, 2], F32)
    LB0 = nc.alloc_sbuf_tensor("LB0", [128, 16], F32)
    LB1 = nc.alloc_sbuf_tensor("LB1", [128, 16], F32)
    GACC = nc.alloc_sbuf_tensor("GACC", [128, 4], F32)
    GACCT = nc.alloc_sbuf_tensor("GACCT", [128, 1], F32)
    GACCS = nc.alloc_sbuf_tensor("GACCS", [128, 132], F32)
    GDUM = nc.alloc_sbuf_tensor("GDUM", [128, 64], BF16)
    RG = nc.alloc_sbuf_tensor("RG", [128, 3], F32)
    DVS = nc.alloc_sbuf_tensor("DVS", [1, 8], F32)
    GROW = nc.alloc_sbuf_tensor("GROW", [1, 16], F32)
    W1B = nc.alloc_sbuf_tensor("W1B", [128, D], BF16)
    EPSC = nc.alloc_sbuf_tensor("EPSC", [128, 1], F32)
    assert nc.sbuf_bytes_remaining > 1024, nc.sbuf_bytes_remaining

    P0 = nc.alloc_psum_tensor("P0", [128, HT], F32)
    P1 = nc.alloc_psum_tensor("P1", [128, HT], F32)
    P2 = nc.alloc_psum_tensor("P2", [128, HT], F32)
    P3 = nc.alloc_psum_tensor("P3", [128, HT], F32)

    S = Sched()

    def d(eng, sem, out, in_, waits=()):
        return S.dma(eng, sem, (lambda e, out=out, in_=in_: e.dma_start(out=out, in_=in_)),
                     waits)

    _nold = [0]

    def mm_nold(e, out, lhsT, rhs, start, stop):
        with nc.discard():
            x = e.matmul(out, lhsT, rhs, start=start, stop=stop).ins
        _nold[0] += 1
        y = mybir.InstMatmult(
            name=f"mmnl{_nold[0]}", engine=x.engine, ins=list(x.ins),
            outs=list(x.outs),
            replication_resolution=x.replication_resolution,
            replication_shift_amnt=x.replication_shift_amnt,
            replication_num_rows=x.replication_num_rows,
            start_tensor_calc=x.start_tensor_calc,
            stop_tensor_calc=x.stop_tensor_calc,
            ldweights=False)
        return e.add_instruction(y)

    for out, in_ in ((DDIAG[:], c_ddiag), (DOFF[:], c_doff), (TRI[:], c_tri),
                     (OFFS[:], c_offs), (ONESRB[:], c_ones_r_bf), (ONESCB[:], c_ones_c_bf),
                     (ONESRF[:], c_ones_r_f), (ONESCF[:], c_ones_c_f), (W0C[:], w0c_d),
                     (BIASC[:], biasc_d), (DVS[0:1, 0:2], dv_d), (W1ROWF[:], w1row_d)):
        d("sync", "qs", out, in_)
    q_setup = S.cnt["qs"]

    def x_dma(h):
        in_ = x_fm[:, h * HT:(h + 1) * HT].rearrange("(c p) n -> p c n", p=128)
        out = XB[:].rearrange("p (c n) -> p c n", c=16)
        return S.dma("gpsimd", "qx", lambda e, o=out, i=in_: e.dma_start(out=o, in_=i))

    qg_x0 = x_dma(0)

    ev_z = S.emit("vector", lambda e: e.memset(GACCS[:], 0.0), inc="ev")
    ev_z = S.emit("vector", lambda e: e.memset(GACC[:], 0.0), waits=[("ev", ev_z)], inc="ev")
    ev_eps = S.emit("vector", lambda e: e.memset(EPSC[:], LN_EPS), inc="ev")
    gjobs = [(mat, sl) for sl in range(NCH) for mat in range(3)]
    gstate = {"next": 0, "cons": {}}
    gv_done = {}
    gsrcs = (w1t, w2t, w3t)

    def emit_gamma_jobs(n):
        while n > 0 and gstate["next"] < len(gjobs):
            idx = gstate["next"]
            gstate["next"] += 1
            n -= 1
            mat, sl = gjobs[idx]
            sem = f"qg{mat + 1}" + ("a" if sl % 2 == 0 else "b")
            w = []
            if (mat, sl - 2) in gstate["cons"]:
                w.append(gstate["cons"][(mat, sl - 2)])
            slot = mat * 2 + (sl % 2)
            seg = GRB[:, slot * 2048:(slot + 1) * 2048]
            cq = S.dma("gpsimd", sem,
                       lambda e, seg=seg, mat=mat, sl=sl: e.dma_start(
                           out=seg.rearrange("p (c n) -> p c n", c=16),
                           in_=gsrcs[mat][sl]),
                       waits=w)
            col = GACCS[:, mat * NCH + sl:mat * NCH + sl + 1]
            if sl % 3 == 2:
                cc = S.emit("vector",
                            lambda e, seg=seg, col=col: e.reduce_sum(
                                col, seg, axis=mybir.AxisListType.X,
                                apply_absolute_value=True),
                            waits=[(sem, cq), ("ev", ev_z)], inc="ev")
                gstate["cons"][(mat, sl)] = ("ev", cc)
            else:
                cc = S.emit("scalar",
                            lambda e, seg=seg, col=col: e.activation(
                                seg, seg, AF.Abs, accum_out=col),
                            waits=[(sem, cq), ("ev", ev_z)], inc="es")
                gstate["cons"][(mat, sl)] = ("es", cc)

    def finish_gamma_scan():
        emit_gamma_jobs(len(gjobs))
        p1 = S.emit("scalar", lambda e: e.copy(GACCT[:], GACCS[:, 0:1]), inc="es")
        p2 = S.emit("scalar", lambda e: e.copy(GACCT[:], GACCS[:, 1:2]),
                    waits=[("es", p1)], inc="es")
        for mat in range(3):
            last = [gstate["cons"][(mat, sl)] for sl in range(NCH)] + [("es", p2)]
            gv_done[mat] = S.emit(
                "vector",
                lambda e, mat=mat: e.reduce_sum(
                    GACC[:, mat:mat + 1], GACCS[:, mat * NCH:(mat + 1) * NCH],
                    axis=mybir.AxisListType.X),
                waits=last, inc="ev")

    ev_dv = S.emit("vector",
                   lambda e: e.tensor_scalar(DVS[0:1, 0:2], DVS[0:1, 0:2], 0.9, 1.0,
                                             ALU.max, ALU.min),
                   waits=[("qs", q_setup)], inc="ev")
    es_ln = S.emit("scalar",
                   lambda e: e.activation(DVS[0:1, 2:4], DVS[0:1, 0:2], AF.Ln),
                   waits=[("ev", ev_dv)], inc="es")
    ep_dvb = S.emit("tensor",
                    lambda e: e.matmul(P0[:, 0:2], ONESRF[:], DVS[0:1, 2:4],
                                       start=True, stop=True),
                    waits=[("es", es_ln)], inc="ep")
    es_dvc = S.emit("scalar", lambda e: e.copy(LNDV[:], P0[:, 0:2]),
                    waits=[("ep", ep_dvb)], inc="es")
    ev_lb = None
    for i, LB in enumerate((LB0, LB1)):
        ev_lb = S.emit("vector",
                       lambda e, LB=LB, i=i: e.tensor_scalar(LB[:], OFFS[:],
                                                             LNDV[:, i:i + 1], None, ALU.mult),
                       waits=[("es", es_dvc)], inc="ev")

    def setup_half(h):
        if h == 1:
            qW = d("sync", "qs", W1ROWF[:], w1row_d)
            w = [("qs", qW)]
        else:
            w = [("qs", q_setup)]
        es_wr = S.emit("scalar", lambda e: e.copy(W1ROWBF[:], W1ROWF[:]),
                       waits=w, inc="es")
        ep_w = None
        for half in range(2):
            P = (P0, P1)[half]
            for s in range(2):
                first = (half == 0 and s == 0)
                ep_w = S.emit("tensor",
                              lambda e, P=P, half=half, s=s, first=first: (
                                  e.matmul if first else (
                                      lambda o, l, r, start, stop: mm_nold(
                                          e, o, l, r, start, stop)))(
                                  P[:, s * 512:(s + 1) * 512], ONESRB[:],
                                  W1ROWBF[0:1, half * 1024 + s * 512:half * 1024 + (s + 1) * 512],
                                  start=True, stop=True),
                              waits=[("es", es_wr)], inc="ep")
        es_wb = None
        for half in range(2):
            P = (P0, P1)[half]
            es_wb = S.emit("scalar",
                           lambda e, P=P, half=half: e.copy(
                               W1B[:, half * 1024:(half + 1) * 1024], P[:]),
                           waits=[("ep", ep_w)], inc="es")
        es_exp = None
        for m in range(16):
            DDm = DDIAG if m == 0 else DOFF
            es_exp = S.emit("scalar",
                            lambda e, m=m, DDm=DDm: e.activation(
                                EXPA[:, m * 128:(m + 1) * 128], DDm[:], AF.Exp,
                                bias=LB0[:, m:m + 1], scale=LNDV[:, 0:1]),
                            waits=[("ev", ev_lb), ("es", es_wb)], inc="es")
            es_exp = S.emit("scalar",
                            lambda e, m=m, DDm=DDm: e.activation(
                                EXPC[:, m * 128:(m + 1) * 128], DDm[:], AF.Exp,
                                bias=LB1[:, m:m + 1], scale=LNDV[:, 1:2]),
                            inc="es")
        return es_exp, es_wb

    def phase_A(h, es_setup, es_wb, qg_x):
        tile_build = {}
        tile_done = {}
        evac = {}
        stats_done = {}
        gctr = [0]

        def mixer_j(j):
            g = gctr[0]
            if USE_BCAST_AP:
                n = j + 1
                in0 = EXPC[:, 0:n * 128].rearrange("p (m q) -> p m q", m=n)
                in1 = W1B[:, j * 128:(j + 1) * 128].rearrange(
                    "p (o q) -> p o q", o=1).broadcast_to([128, n, 128])
                o3 = C2ROW[:, 0:n * 128].rearrange("p (m q) -> p m q", m=n)
                S.emit("vector",
                       lambda e, o3=o3, in0=in0, in1=in1: e.tensor_tensor(o3, in0, in1,
                                                                          ALU.mult),
                       waits=[("es", es_setup)])
            else:
                for m in range(j + 1):
                    S.emit("vector",
                           lambda e, m=m, j=j: e.tensor_tensor(
                               C2ROW[:, m * 128:(m + 1) * 128],
                               EXPC[:, m * 128:(m + 1) * 128],
                               W1B[:, j * 128:(j + 1) * 128], ALU.mult),
                           waits=[("es", es_setup)])
            P = (P0, P1)[j % 2]
            for k in range(j + 1):
                m = j - k
                slot = g % 17
                w = [("es", es_setup)]
                if g - 17 in tile_done:
                    w.append(("ep", tile_done[g - 17]))
                cv = S.emit("vector",
                            lambda e, m=m, k=k, slot=slot: e.scalar_tensor_tensor(
                                MRING[:, slot * 128:(slot + 1) * 128],
                                EXPA[:, m * 128:(m + 1) * 128], W0C[:, k:k + 1],
                                C2ROW[:, m * 128:(m + 1) * 128], ALU.mult, ALU.add),
                            waits=w, inc="ev")
                if k == j:
                    cv = S.emit("vector",
                                lambda e, slot=slot: e.tensor_tensor(
                                    MRING[:, slot * 128:(slot + 1) * 128],
                                    MRING[:, slot * 128:(slot + 1) * 128], TRI[:], ALU.mult),
                                inc="ev")
                tile_build[g] = cv
                g += 1
            for k in range(j + 1):
                gk = g - (j + 1) + k
                slot = gk % 17
                w = []
                if k == 0:
                    w = [("ev", tile_build[g - 1]), ("qx", qg_x), ("es", es_wb)]
                    if (j - 2) in evac:
                        w.append(("ev", evac[j - 2]))
                S.emit("tensor",
                       lambda e, P=P, slot=slot, k=k, j=j: e.matmul(
                           P[:, 0:512], MRING[:, slot * 128:(slot + 1) * 128],
                           XB[:, k * HT:k * HT + 512], start=(k == 0), stop=(k == j)),
                       waits=w)
                cp = S.emit("tensor",
                            lambda e, P=P, slot=slot, k=k, j=j: mm_nold(
                                e, P[:, 512:1024], MRING[:, slot * 128:(slot + 1) * 128],
                                XB[:, k * HT + 512:(k + 1) * HT], (k == 0), (k == j)),
                            inc="ep")
                tile_done[gk] = cp
            gctr[0] = g

        def post_j(j):
            P = (P0, P1)[j % 2]
            cv = S.emit("vector",
                        lambda e, P=P, j=j: e.scalar_tensor_tensor(
                            H1[:, j * HT:(j + 1) * HT], P[:], BIASC[:, j:j + 1],
                            XB[:, j * HT:(j + 1) * HT], ALU.add, ALU.add),
                        waits=[("ep", tile_done[(j + 1) * (j + 2) // 2 - 1]),
                               ("qx", qg_x)], inc="ev")
            evac[j] = cv
            w = [("ev", cv)]
            if j - 1 in stats_done:
                w.append(("ep", stats_done[j - 1]))
            cs = S.emit("scalar",
                        lambda e, j=j: e.activation(SQ[:], H1[:, j * HT:(j + 1) * HT],
                                                    AF.Square),
                        waits=w, inc="es")
            cp2 = None
            for s in range(2):
                S.emit("tensor",
                       lambda e, j=j, s=s: (e.matmul(
                           P2[0:1, s * 512:(s + 1) * 512], ONESCB[:],
                           H1[:, j * HT + s * 512:j * HT + (s + 1) * 512],
                           start=(j == 0), stop=(j == 15)) if s == 0 else mm_nold(
                           e, P2[0:1, s * 512:(s + 1) * 512], ONESCB[:],
                           H1[:, j * HT + s * 512:j * HT + (s + 1) * 512],
                           (j == 0), (j == 15))),
                       waits=[("ev", cv)])
                cp2 = S.emit("tensor",
                             lambda e, j=j, s=s: mm_nold(
                                 e, P3[0:1, s * 512:(s + 1) * 512], ONESCB[:],
                                 SQ[:, s * 512:(s + 1) * 512],
                                 (j == 0), (j == 15)),
                             waits=[("es", cs)], inc="ep")
            stats_done[j] = cp2

        for j in range(16):
            mixer_j(j)
            if j >= 1:
                post_j(j - 1)
            if h == 0:
                emit_gamma_jobs(9)
        post_j(15)
        return stats_done[15], evac

    def ln_finalize(stats_ep, last_z_ev):
        c1 = S.emit("vector",
                    lambda e: e.tensor_scalar(MROWF[:], P2[0:1, :], 1.0 / D, 0.0,
                                              ALU.mult, ALU.add),
                    waits=[("ep", stats_ep), ("ev", last_z_ev)], inc="ev")
        S.emit("vector",
               lambda e: e.tensor_tensor(VROW[:], MROWF[:], MROWF[:], ALU.mult))
        c3 = S.emit("vector",
                    lambda e: e.scalar_tensor_tensor(VROW[:], P3[0:1, :], 1.0 / D,
                                                     VROW[:], ALU.mult, ALU.subtract),
                    inc="ev")
        S.emit("scalar",
               lambda e: e.activation(VROW[:], VROW[:], AF.Ln, bias=EPSC[0:1, 0:1]),
               waits=[("ev", c3), ("ev", ev_eps)])
        c5 = S.emit("scalar",
                    lambda e: e.activation(RROW[:], VROW[:], AF.Exp, scale=-0.5),
                    inc="es")
        c6 = S.emit("vector",
                    lambda e: e.scalar_tensor_tensor(BROW[:], MROWF[:], -1.0, RROW[:],
                                                     ALU.mult, ALU.mult),
                    waits=[("es", c5)], inc="ev")
        cp = None
        for s in range(2):
            S.emit("tensor",
                   lambda e, s=s: (e.matmul(P0[:, s * 512:(s + 1) * 512], ONESRB[:],
                                            RROW[0:1, s * 512:(s + 1) * 512],
                                            start=True, stop=True) if s == 0 else
                                   mm_nold(e, P0[:, s * 512:(s + 1) * 512], ONESRB[:],
                                           RROW[0:1, s * 512:(s + 1) * 512], True, True)),
                   waits=[("ev", c6)])
            cp = S.emit("tensor",
                        lambda e, s=s: mm_nold(
                            e, P1[:, s * 512:(s + 1) * 512], ONESRB[:],
                            BROW[0:1, s * 512:(s + 1) * 512], True, True),
                        inc="ep")
        S.emit("scalar", lambda e: e.copy(RB[:], P0[:]), waits=[("ep", cp)])
        c8 = S.emit("scalar", lambda e: e.copy(BB[:], P1[:]), inc="es")
        return c8

    def gamma_finalize():
        w = [("ev", gv_done[m]) for m in range(3)]
        cp = S.emit("tensor",
                    lambda e: e.matmul(P2[0:1, 0:3], ONESCF[:], GACC[:, 0:3],
                                       start=True, stop=True),
                    waits=w, inc="ep")
        c1 = S.emit("vector",
                    lambda e: e.tensor_scalar(GROW[0:1, 0:3], P2[0:1, 0:3], 1.0 / NW,
                                              G_EPS, ALU.mult, ALU.add),
                    waits=[("ep", cp)], inc="ev")
        c2 = S.emit("vector",
                    lambda e: e.reciprocal(GROW[0:1, 4:7], GROW[0:1, 0:3]),
                    waits=[("ev", c1)], inc="ev")
        cp2 = S.emit("tensor",
                     lambda e: e.matmul(P2[:, 0:3], ONESRF[:], GROW[0:1, 4:7],
                                        start=True, stop=True),
                     waits=[("ev", c2)], inc="ep")
        c3 = S.emit("scalar", lambda e: e.copy(RG[:], P2[:, 0:3]),
                    waits=[("ep", cp2)], inc="es")
        return c3

    def normalize_bf16(ln_es):
        last = None
        for j in range(16):
            S.emit("vector",
                   lambda e, j=j: e.tensor_tensor(
                       H1[:, j * HT:(j + 1) * HT], H1[:, j * HT:(j + 1) * HT],
                       RB[:], ALU.mult),
                   waits=[("es", ln_es)])
            last = S.emit("vector",
                          lambda e, j=j: e.tensor_tensor(
                              H1[:, j * HT:(j + 1) * HT], H1[:, j * HT:(j + 1) * HT],
                              BB[:], ALU.add),
                          inc="ev")
        return last

    def normalize_store(ln_es, h):
        stage_dma = {}
        for j in range(16):
            S.emit("vector",
                   lambda e, j=j: e.tensor_tensor(
                       TNORM[:], H1[:, j * HT:(j + 1) * HT], RB[:], ALU.mult),
                   waits=[("es", ln_es)])
            for s in range(2):
                u = j * 2 + s
                STG = (STAGEA, STAGEB)[u % 2]
                par = "qST" + ("a" if u % 2 == 0 else "b")
                w = []
                if u - 2 in stage_dma:
                    w.append(stage_dma[u - 2])
                cv = S.emit("vector",
                            lambda e, s=s, STG=STG: e.tensor_tensor(
                                STG[:], TNORM[:, s * 512:(s + 1) * 512],
                                BB[:, s * 512:(s + 1) * 512], ALU.add),
                            waits=w, inc="ev")
                stage_dma[u] = (par, S.dma(
                    "sync", par,
                    lambda e, j=j, s=s, STG=STG: e.dma_start(
                        out=out_fm[j * 128:(j + 1) * 128,
                                   h * HT + s * 512:h * HT + (s + 1) * 512],
                        in_=STG[:]),
                    waits=[("ev", cv)]))

    def phase_B(h, rg_es, norm_ev):
        mult_done = {}
        sig_done = {}
        pe_done = {}
        gp_done = {}
        q_ready_d = {}
        QMAT = {0: (L1, Q1R, w1t, q1c, "sync", "qL1", "qc1"),
                1: (L2, Q2R, w2t, q2c, "scalar", "qL2", "qc2"),
                2: (L3, Q3S, w3t, q3c, "gpsimd", "qL3", "qc3")}

        def quant(mat, hb):
            L, QOUT, src_t, qcache, eng_q, semL, semC = QMAT[mat]
            hs = hb % 2
            for piece in range(2):
                u = 2 * hb + piece
                slot = u % 2
                sem = semL + ("a" if slot == 0 else "b")
                seg = L[:, slot * 1024:(slot + 1) * 1024]
                w = []
                if (mat, u - 2) in gp_done:
                    w.append(("ev", gp_done[(mat, u - 2)]))
                cq = S.dma(eng_q, sem,
                           lambda e, seg=seg, src=src_t[hb], piece=piece: e.dma_start(
                               out=seg.rearrange("p (c n) -> p c n", c=8),
                               in_=src[:, piece * 8:(piece + 1) * 8, :]),
                           waits=w)
                cs = S.emit("scalar",
                            lambda e, seg=seg, mat=mat: e.activation(
                                seg, seg, AF.Copy, bias=MAGIC, scale=RG[:, mat:mat + 1]),
                            waits=[(sem, cq), ("es", rg_es)], inc="es")
                S.emit("vector",
                       lambda e, seg=seg: e.tensor_scalar(seg, seg, CM1, CP1,
                                                          ALU.max, ALU.min),
                       waits=[("es", cs)])
                w = [(semC, S.cnt[semC])]
                if hb - 2 in pe_done and mat < 2:
                    w.append(("ep", pe_done[hb - 2]))
                qcol = hs * D + piece * 1024
                cg = S.emit("vector",
                            lambda e, seg=seg, QOUT=QOUT, qcol=qcol: e.tensor_scalar(
                                QOUT[:, qcol:qcol + 1024], seg, MAGIC, None, ALU.subtract),
                            waits=w, inc="ev")
                gp_done[(mat, u)] = cg
            co = (q3c[hb].rearrange("p c n -> p (c n)") if mat == 2 else qcache[hb])
            S.dma(eng_q, semC,
                  lambda e, co=co, QOUT=QOUT, hs=hs: e.dma_start(
                      out=co, in_=QOUT[:, hs * D:(hs + 1) * D]),
                  waits=[("ev", gp_done[(mat, 2 * hb + 1)])])

        def produce(hb):
            hs = hb % 2
            if h == 0:
                for mat in range(3):
                    quant(mat, hb)
            else:
                w = []
                if hb - 2 in pe_done:
                    w.append(("ep", pe_done[hb - 2]))
                par = "a" if hs == 0 else "b"
                c1 = S.dma("sync", "qR1" + par,
                           lambda e, hs=hs, hb=hb: e.dma_start(
                               out=Q1R[:, hs * D:(hs + 1) * D], in_=q1c[hb]), w)
                c2 = S.dma("scalar", "qR2" + par,
                           lambda e, hs=hs, hb=hb: e.dma_start(
                               out=Q2R[:, hs * D:(hs + 1) * D], in_=q2c[hb]), w)
                q_ready_d[hb] = [("qR1" + par, c1), ("qR2" + par, c2)]

        def consume(hb):
            hs = hb % 2
            if h == 0:
                q_ready = [("ev", gp_done[(0, 2 * hb + 1)]),
                           ("ev", gp_done[(1, 2 * hb + 1)])]
            else:
                q_ready = q_ready_d[hb]
            PA, PB = (P0, P1) if hs == 0 else (P2, P3)
            w = list(q_ready) + [("ev", norm_ev)]
            if hb - 2 in mult_done:
                w.append(("ev", mult_done[hb - 2]))
            cp = None
            for QR, PX in ((Q1R, PA), (Q2R, PB)):
                for dc in range(NCD):
                    for s in range(2):
                        last = (QR is Q2R and dc == NCD - 1 and s == 1)
                        cp = S.emit("tensor",
                                    lambda e, QR=QR, PX=PX, hs=hs, dc=dc, s=s: (e.matmul(
                                        PX[:, s * 512:(s + 1) * 512],
                                        QR[:, hs * D + dc * 128:hs * D + (dc + 1) * 128],
                                        H1[:, dc * HT + s * 512:dc * HT + (s + 1) * 512],
                                        start=(dc == 0), stop=(dc == NCD - 1)) if s == 0
                                        else mm_nold(
                                        e, PX[:, s * 512:(s + 1) * 512],
                                        QR[:, hs * D + dc * 128:hs * D + (dc + 1) * 128],
                                        H1[:, dc * HT + s * 512:dc * HT + (s + 1) * 512],
                                        (dc == 0), (dc == NCD - 1))),
                                    waits=(w if QR is Q1R and dc == 0 and s == 0 else ()),
                                    inc=("ep" if last else None))
            pe_done[hb] = cp
            w = [("ep", cp)]
            if hb - 1 in mult_done:
                w.append(("ev", mult_done[hb - 1]))
            cs = S.emit("scalar",
                        lambda e, PA=PA: e.activation(SIG[:], PA[:], AF.Sigmoid),
                        waits=w, inc="es")
            sig_done[hb] = cs
            cv = S.emit("vector",
                        lambda e, PB=PB, hb=hb: e.tensor_tensor(
                            HID[:, hb * HT:(hb + 1) * HT], PB[:], SIG[:], ALU.mult),
                        waits=[("es", cs)], inc="ev")
            mult_done[hb] = cv

        for hb in range(NCH):
            produce(hb)
            if hb >= 1:
                consume(hb - 1)
        consume(NCH - 1)
        return mult_done[NCH - 1]

    def phase_C(h, hid_ev):
        evac = {}
        stats_done = {}
        pe_f = {}

        col_cnt = {}

        def col_dma(dco):
            slot = dco % 2
            w = []
            if dco - 2 in pe_f:
                w.append(("ep", pe_f[dco - 2]))
            col_cnt[dco] = S.dma(
                "sync", "qCC" + ("a" if slot == 0 else "b"),
                lambda e, slot=slot, dco=dco: e.dma_start(
                    out=QC[:, slot * 5632:(slot + 1) * 5632].rearrange(
                        "p (a b) -> p a b", a=NCH),
                    in_=q3c[:, :, dco, :].rearrange("a p b -> p a b")),
                waits=w)

        def cpe(dco):
            slot = dco % 2
            P = (P0, P1)[slot]
            w = [("qCC" + ("a" if slot == 0 else "b"), col_cnt[dco]), ("ev", hid_ev)]
            if dco - 2 in evac:
                w.append(("ev", evac[dco - 2]))
            cp = None
            for k in range(NCH):
                for s in range(2):
                    last = (k == NCH - 1 and s == 1)
                    cp = S.emit("tensor",
                                lambda e, P=P, slot=slot, k=k, s=s: (e.matmul(
                                    P[:, s * 512:(s + 1) * 512],
                                    QC[:, slot * 5632 + k * 128:slot * 5632 + (k + 1) * 128],
                                    HID[:, k * HT + s * 512:k * HT + (s + 1) * 512],
                                    start=(k == 0), stop=(k == NCH - 1)) if s == 0
                                    else mm_nold(
                                    e, P[:, s * 512:(s + 1) * 512],
                                    QC[:, slot * 5632 + k * 128:slot * 5632 + (k + 1) * 128],
                                    HID[:, k * HT + s * 512:k * HT + (s + 1) * 512],
                                    (k == 0), (k == NCH - 1))),
                                waits=(w if k == 0 and s == 0 else ()),
                                inc=("ep" if last else None))
            pe_f[dco] = cp

        def cpost(dco):
            P = (P0, P1)[dco % 2]
            cv = S.emit("vector",
                        lambda e, P=P, dco=dco: e.tensor_tensor(
                            H1[:, dco * HT:(dco + 1) * HT], P[:],
                            H1[:, dco * HT:(dco + 1) * HT], ALU.add),
                        waits=[("ep", pe_f[dco])], inc="ev")
            evac[dco] = cv
            w = [("ev", cv)]
            if dco - 1 in stats_done:
                w.append(("ep", stats_done[dco - 1]))
            cs = S.emit("scalar",
                        lambda e, dco=dco: e.activation(SQ[:], H1[:, dco * HT:(dco + 1) * HT],
                                                        AF.Square),
                        waits=w, inc="es")
            cp2 = None
            for s in range(2):
                S.emit("tensor",
                       lambda e, dco=dco, s=s: (e.matmul(
                           P2[0:1, s * 512:(s + 1) * 512], ONESCB[:],
                           H1[:, dco * HT + s * 512:dco * HT + (s + 1) * 512],
                           start=(dco == 0), stop=(dco == NCD - 1)) if s == 0 else
                           mm_nold(
                           e, P2[0:1, s * 512:(s + 1) * 512], ONESCB[:],
                           H1[:, dco * HT + s * 512:dco * HT + (s + 1) * 512],
                           (dco == 0), (dco == NCD - 1))),
                       waits=[("ev", cv)])
                cp2 = S.emit("tensor",
                             lambda e, dco=dco, s=s: mm_nold(
                                 e, P3[0:1, s * 512:(s + 1) * 512], ONESCB[:],
                                 SQ[:, s * 512:(s + 1) * 512],
                                 (dco == 0), (dco == NCD - 1)),
                             waits=[("es", cs)], inc="ep")
            stats_done[dco] = cp2

        for dco in range(NCD):
            col_dma(dco)
            if dco >= 1:
                cpe(dco - 1)
            if dco >= 2:
                cpost(dco - 2)
        cpe(NCD - 1)
        cpost(NCD - 2)
        cpost(NCD - 1)
        ln_es = ln_finalize(stats_done[NCD - 1], evac[NCD - 1])
        normalize_store(ln_es, h)

    es_setup, es_wb = setup_half(0)
    stats_ep, evacA = phase_A(0, es_setup, es_wb, qg_x0)
    ln_es = ln_finalize(stats_ep, evacA[15])
    norm_ev = normalize_bf16(ln_es)
    if dump_h1:
        d("sync", "qs", dbg_h1[0], H1[:], waits=[("ev", norm_ev)])
    if phases >= 2:
        finish_gamma_scan()
        rg_es = gamma_finalize()
        S.barrier()
        hid_ev = phase_B(0, rg_es, norm_ev)
        if dump_hid:
            d("sync", "qs", dbg_hid[0], HID[:], waits=[("ev", hid_ev)])
        if dump_q:
            S.barrier()
            d("sync", "qs", dbg_q1[:], q1c)
            d("sync", "qs", dbg_q2[:], q2c)
            d("sync", "qs", dbg_rg, RG[:])
            d("sync", "qs", dbg_ga, GACC[:])
    if phases >= 3:
        S.barrier()
        phase_C(0, hid_ev)
        S.barrier()
        qg_x1 = x_dma(1)
        es_setup2, es_wb2 = setup_half(1)
        stats_ep2, evacA2 = phase_A(1, es_setup2, es_wb2, qg_x1)
        ln_es2 = ln_finalize(stats_ep2, evacA2[15])
        norm_ev2 = normalize_bf16(ln_es2)
        if dump_h1:
            d("sync", "qs", dbg_h1[1], H1[:], waits=[("ev", norm_ev2)])
        S.barrier()
        hid_ev2 = phase_B(1, rg_es, norm_ev2)
        if dump_hid:
            d("sync", "qs", dbg_hid[1], HID[:], waits=[("ev", hid_ev2)])
        S.barrier()
        phase_C(1, hid_ev2)

    qfin = [(n, S.cnt[n]) for n in S.cnt if n.startswith("q")]
    S.emit("sync", lambda e: e.nop(), waits=qfin)

    with contextlib.ExitStack() as stack:
        sems = {n: stack.enter_context(nc.semaphore(n)) for n in S.cnt}
        block = stack.enter_context(nc.Block())

        def make_body(eng_name):
            def body(e):
                for waits, fn, inc, amt in S.ops[eng_name]:
                    for s, v in waits:
                        e.wait_ge(sems[s], v)
                    ins = fn(e)
                    if inc is not None:
                        ins.then_inc(sems[inc], amt)
            return body

        for eng_name in ENGS:
            getattr(block, eng_name)(make_body(eng_name))
    return nc


def _host_layout(L0_attn_x, L0_attn_weight, L0_attn_bias, L0_attn_decay_value,
                 L0_ffn_w1, L0_ffn_w2, L0_ffn_w3):
    f32 = np.float32
    x2 = np.ascontiguousarray(np.asarray(L0_attn_x, f32).reshape(8 * TOK, D))
    w1t = np.ascontiguousarray(np.asarray(L0_ffn_w1, f32).reshape(NCH, 128, NCD, 128)
                               .transpose(0, 3, 2, 1))
    w2t = np.ascontiguousarray(np.asarray(L0_ffn_w2, f32).reshape(NCH, 128, NCD, 128)
                               .transpose(0, 3, 2, 1))
    w3t = np.ascontiguousarray(np.asarray(L0_ffn_w3, f32).reshape(NCD, 128, NCH, 128)
                               .transpose(2, 3, 0, 1))
    wt = np.asarray(L0_attn_weight, f32)
    w0c = np.ascontiguousarray(wt[0].reshape(NCD, 128).T)
    w1row = np.ascontiguousarray(wt[1].reshape(1, D))
    biasc = np.ascontiguousarray(np.asarray(L0_attn_bias, f32).reshape(NCD, 128).T)
    dv = np.ascontiguousarray(np.asarray(L0_attn_decay_value, f32).reshape(1, 2))

    ii = np.arange(128, dtype=f32)
    dmat = ii[None, :] - ii[:, None]
    shared = {
        "c_ddiag": np.ascontiguousarray(np.maximum(dmat, 0.0).astype(f32)),
        "c_doff": np.ascontiguousarray(dmat.astype(f32)),
        "c_tri": np.ascontiguousarray((dmat >= 0).astype(ml_dtypes.bfloat16)),
        "c_offs": np.ascontiguousarray(
            np.broadcast_to(128.0 * np.arange(16, dtype=f32), (128, 16))),
        "c_ones_r_bf": np.ones((1, 128), ml_dtypes.bfloat16),
        "c_ones_c_bf": np.ones((128, 1), ml_dtypes.bfloat16),
        "c_ones_r_f": np.ones((1, 128), f32),
        "c_ones_c_f": np.ones((128, 1), f32),
        "w1t": w1t, "w2t": w2t, "w3t": w3t, "w0c": w0c, "biasc": biasc,
        "dv": dv, "w1row": w1row,
    }
    in_maps = []
    for c in range(8):
        m = dict(shared)
        m["x_fm"] = np.ascontiguousarray(x2[c * TOK:(c + 1) * TOK, :].T)
        in_maps.append(m)
    return in_maps


def kernel(L0_attn_x, L0_attn_weight, L0_attn_bias, L0_attn_decay_value,
           L0_ffn_w1, L0_ffn_w2, L0_ffn_w3):
    in_maps = _host_layout(L0_attn_x, L0_attn_weight, L0_attn_bias,
                           L0_attn_decay_value, L0_ffn_w1, L0_ffn_w2, L0_ffn_w3)
    nc = build_program()
    res = run_bass_kernel_spmd(nc, in_maps, core_ids=list(range(8)))
    parts = [np.asarray(r["out_fm"]).T for r in res.results]
    return np.ascontiguousarray(np.concatenate(parts, axis=0)
                                .reshape(2, 8192, D)).astype(np.float32)


# version 18
# speedup vs baseline: 1.6400x; 1.1026x over previous
import contextlib
import numpy as np
import ml_dtypes
import concourse.bass as bass
import concourse.mybir as mybir
from concourse.bass_utils import run_bass_kernel_spmd

F32 = mybir.dt.float32
BF16 = mybir.dt.bfloat16
AF = mybir.ActivationFunctionType
ALU = mybir.AluOpType

MAGIC = float(np.float32(1.5 * 2 ** 23))
CM1 = float(np.float32(MAGIC - 1.0))
CP1 = float(np.float32(MAGIC + 1.0))
LN_EPS = 1e-5
G_EPS = 1e-8
NW = 5632 * 2048
D = 2048
H = 5632
NCD = 16
NCH = 44
TOK = 2048
HT = 1024
ENGS = ("sync", "gpsimd", "vector", "scalar", "tensor")

USE_BCAST_AP = True


class Sched:

    def __init__(self):
        self.ops = {e: [] for e in ENGS}
        names = ["qs", "qa", "qg", "qx", "qc1", "qc2", "qc3",
                 "ep", "ev", "es", "eg"]
        for b in ("qg1", "qg2", "qg3", "qL1", "qL2", "qL3", "qR1", "qR2",
                  "qCC", "qST"):
            names += [b + "a", b + "b"]
        self.cnt = {n: 0 for n in names}
        self.hw = {e: {} for e in ENGS}

    def emit(self, eng, fn, waits=(), inc=None, amt=1):
        w = []
        for s, v in waits:
            if v is None or v <= 0:
                continue
            assert v <= self.cnt[s], f"wait {s}>={v} but only {self.cnt[s]} scheduled"
            if self.hw[eng].get(s, 0) >= v:
                continue
            self.hw[eng][s] = v
            w.append((s, v))
        val = None
        if inc is not None:
            self.cnt[inc] += amt
            val = self.cnt[inc]
        self.ops[eng].append((tuple(w), fn, inc, amt))
        return val

    def dma(self, eng, sem, fn, waits=()):
        return self.emit(eng, fn, waits, inc=sem, amt=16)

    def barrier(self):
        snap = [(s, v) for s, v in self.cnt.items()]
        for eng, sem in (("tensor", "ep"), ("vector", "ev"),
                         ("scalar", "es"), ("gpsimd", "eg")):
            self.emit(eng, lambda e: e.nop(), waits=snap, inc=sem)
        snap2 = [(s, v) for s, v in self.cnt.items()]
        for eng in ENGS:
            self.emit(eng, lambda e: e.nop(), waits=snap2)


def build_program(phases=3, dump_h1=False, dump_hid=False, dump_q=False):
    nc = bass.Bass("TRN2", target_bir_lowering=False)

    x_fm = nc.dram_tensor("x_fm", [D, TOK], F32, kind="ExternalInput").ap()
    w1t = nc.dram_tensor("w1t", [NCH, 128, NCD, 128], F32, kind="ExternalInput").ap()
    w2t = nc.dram_tensor("w2t", [NCH, 128, NCD, 128], F32, kind="ExternalInput").ap()
    w3t = nc.dram_tensor("w3t", [NCH, 128, NCD, 128], F32, kind="ExternalInput").ap()
    c_ddiag = nc.dram_tensor("c_ddiag", [128, 128], F32, kind="ExternalInput").ap()
    c_doff = nc.dram_tensor("c_doff", [128, 128], F32, kind="ExternalInput").ap()
    c_tri = nc.dram_tensor("c_tri", [128, 128], BF16, kind="ExternalInput").ap()
    c_offs = nc.dram_tensor("c_offs", [128, 16], F32, kind="ExternalInput").ap()
    c_ones_r_bf = nc.dram_tensor("c_ones_r_bf", [1, 128], BF16, kind="ExternalInput").ap()
    c_ones_c_bf = nc.dram_tensor("c_ones_c_bf", [128, 1], BF16, kind="ExternalInput").ap()
    c_ones_r_f = nc.dram_tensor("c_ones_r_f", [1, 128], F32, kind="ExternalInput").ap()
    c_ones_c_f = nc.dram_tensor("c_ones_c_f", [128, 1], F32, kind="ExternalInput").ap()
    w0c_d = nc.dram_tensor("w0c", [128, 16], F32, kind="ExternalInput").ap()
    biasc_d = nc.dram_tensor("biasc", [128, 16], F32, kind="ExternalInput").ap()
    dv_d = nc.dram_tensor("dv", [1, 2], F32, kind="ExternalInput").ap()
    w1row_d = nc.dram_tensor("w1row", [1, D], F32, kind="ExternalInput").ap()

    out_fm = nc.dram_tensor("out_fm", [D, TOK], F32, kind="ExternalOutput").ap()

    q1c = nc.dram_tensor("q1c", [NCH, 128, D], BF16, kind="Internal").ap()
    q2c = nc.dram_tensor("q2c", [NCH, 128, D], BF16, kind="Internal").ap()
    q3c = nc.dram_tensor("q3c", [NCH, 128, NCD, 128], BF16, kind="Internal").ap()

    dbg_h1 = dbg_hid = dbg_q1 = dbg_q2 = dbg_rg = dbg_ga = None
    if dump_h1:
        dbg_h1 = nc.dram_tensor("dbg_h1", [2, 128, 16 * HT], BF16, kind="ExternalOutput").ap()
    if dump_hid:
        dbg_hid = nc.dram_tensor("dbg_hid", [2, 128, NCH * HT], BF16, kind="ExternalOutput").ap()
    if dump_q:
        dbg_q1 = nc.dram_tensor("dbg_q1", [NCH, 128, D], BF16, kind="ExternalOutput").ap()
        dbg_q2 = nc.dram_tensor("dbg_q2", [NCH, 128, D], BF16, kind="ExternalOutput").ap()
        dbg_rg = nc.dram_tensor("dbg_rg", [128, 3], F32, kind="ExternalOutput").ap()
        dbg_ga = nc.dram_tensor("dbg_ga", [128, 4], F32, kind="ExternalOutput").ap()

    REG = 32768 + 32768 + 90112 + 24576 + 20480
    base, _ = nc.bump_sbuf(REG)
    off_x, off_h1, off_hid, off_qr, off_scr = (
        base, base + 32768, base + 65536, base + 155648, base + 180224)

    def at(name, shape, dtype, off):
        return nc.alloc_sbuf_tensor_at(name, shape, dtype, offset=off)

    XB = at("XB", [128, 16 * HT], BF16, off_x)
    L1 = at("L1", [128, 2048], F32, off_x)
    L2 = at("L2", [128, 2048], F32, off_x + 8192)
    L3 = at("L3", [128, 2048], F32, off_x + 16384)
    H1 = at("H1", [128, 16 * HT], BF16, off_h1)
    HID = at("HID", [128, NCH * HT], BF16, off_hid)
    GR = at("GR", [128, 6 * 2048], F32, off_hid)
    GRB = at("GRB", [128, 6 * 2048], BF16, off_hid)
    Q1R = at("Q1R", [128, 2 * D], BF16, off_qr)
    Q2R = at("Q2R", [128, 2 * D], BF16, off_qr + 8192)
    Q3S = at("Q3S", [128, 2 * D], BF16, off_qr + 16384)
    QC = at("QC", [128, 2 * 5632], BF16, off_qr)
    W1ROWF = at("W1ROWF", [1, D], F32, off_qr)
    EXPA = at("EXPA", [128, 2048], BF16, off_scr)
    EXPC = at("EXPC", [128, 2048], BF16, off_scr + 4096)
    C2ROW = at("C2ROW", [128, 2048], BF16, off_scr + 8192)
    VROW = at("VROW", [1, HT], F32, off_scr)
    MROWF = at("MROWF", [1, HT], F32, off_scr + 4096)
    RROW = at("RROW", [1, HT], BF16, off_scr + 8192)
    BROW = at("BROW", [1, HT], BF16, off_scr + 10240)
    STAGEB = at("STAGEB", [128, 512], F32, off_scr)
    RB = at("RB", [128, HT], BF16, off_scr + 12288)
    SIG = at("SIG", [128, HT], BF16, off_scr + 12288)
    BB = at("BB", [128, HT], BF16, off_scr + 14336)
    SQ = at("SQ", [128, HT], BF16, off_scr + 16384)
    TNORM = at("TNORM", [128, HT], BF16, off_scr + 16384)
    MRING = at("MRING", [128, 17 * 128], BF16, off_qr + 8192)
    STAGEA = at("STAGEA", [128, 512], F32, off_scr + 18432)
    W1ROWBF = at("W1ROWBF", [1, D], BF16, off_scr + 8192)

    DDIAG = nc.alloc_sbuf_tensor("DDIAG", [128, 128], F32)
    DOFF = nc.alloc_sbuf_tensor("DOFF", [128, 128], F32)
    TRI = nc.alloc_sbuf_tensor("TRI", [128, 128], BF16)
    OFFS = nc.alloc_sbuf_tensor("OFFS", [128, 16], F32)
    ONESRB = nc.alloc_sbuf_tensor("ONESRB", [1, 128], BF16)
    ONESCB = nc.alloc_sbuf_tensor("ONESCB", [128, 1], BF16)
    ONESRF = nc.alloc_sbuf_tensor("ONESRF", [1, 128], F32)
    ONESCF = nc.alloc_sbuf_tensor("ONESCF", [128, 1], F32)
    W0C = nc.alloc_sbuf_tensor("W0C", [128, 16], F32)
    BIASC = nc.alloc_sbuf_tensor("BIASC", [128, 16], F32)
    LNDV = nc.alloc_sbuf_tensor("LNDV", [128, 2], F32)
    LB0 = nc.alloc_sbuf_tensor("LB0", [128, 16], F32)
    LB1 = nc.alloc_sbuf_tensor("LB1", [128, 16], F32)
    GACC = nc.alloc_sbuf_tensor("GACC", [128, 4], F32)
    GACCT = nc.alloc_sbuf_tensor("GACCT", [128, 1], F32)
    GACCS = nc.alloc_sbuf_tensor("GACCS", [128, 132], F32)
    GDUM = nc.alloc_sbuf_tensor("GDUM", [128, 64], BF16)
    RG = nc.alloc_sbuf_tensor("RG", [128, 3], F32)
    RGN = nc.alloc_sbuf_tensor("RGN", [128, 3], F32)
    DVS = nc.alloc_sbuf_tensor("DVS", [1, 8], F32)
    GROW = nc.alloc_sbuf_tensor("GROW", [1, 16], F32)
    W1B = nc.alloc_sbuf_tensor("W1B", [128, D], BF16)
    EPSC = nc.alloc_sbuf_tensor("EPSC", [128, 1], F32)
    assert nc.sbuf_bytes_remaining > 1024, nc.sbuf_bytes_remaining

    P0 = nc.alloc_psum_tensor("P0", [128, HT], F32)
    P1 = nc.alloc_psum_tensor("P1", [128, HT], F32)
    P2 = nc.alloc_psum_tensor("P2", [128, HT], F32)
    P3 = nc.alloc_psum_tensor("P3", [128, HT], F32)

    S = Sched()

    def d(eng, sem, out, in_, waits=()):
        return S.dma(eng, sem, (lambda e, out=out, in_=in_: e.dma_start(out=out, in_=in_)),
                     waits)

    _nold = [0]

    def mm_nold(e, out, lhsT, rhs, start, stop):
        with nc.discard():
            x = e.matmul(out, lhsT, rhs, start=start, stop=stop).ins
        _nold[0] += 1
        y = mybir.InstMatmult(
            name=f"mmnl{_nold[0]}", engine=x.engine, ins=list(x.ins),
            outs=list(x.outs),
            replication_resolution=x.replication_resolution,
            replication_shift_amnt=x.replication_shift_amnt,
            replication_num_rows=x.replication_num_rows,
            start_tensor_calc=x.start_tensor_calc,
            stop_tensor_calc=x.stop_tensor_calc,
            ldweights=False)
        return e.add_instruction(y)

    for out, in_ in ((DDIAG[:], c_ddiag), (DOFF[:], c_doff), (TRI[:], c_tri),
                     (OFFS[:], c_offs), (ONESRB[:], c_ones_r_bf), (ONESCB[:], c_ones_c_bf),
                     (ONESRF[:], c_ones_r_f), (ONESCF[:], c_ones_c_f), (W0C[:], w0c_d),
                     (BIASC[:], biasc_d), (DVS[0:1, 0:2], dv_d), (W1ROWF[:], w1row_d)):
        d("sync", "qs", out, in_)
    q_setup = S.cnt["qs"]

    def x_dma(h):
        in_ = x_fm[:, h * HT:(h + 1) * HT].rearrange("(c p) n -> p c n", p=128)
        out = XB[:].rearrange("p (c n) -> p c n", c=16)
        return S.dma("gpsimd", "qx", lambda e, o=out, i=in_: e.dma_start(out=o, in_=i))

    qg_x0 = x_dma(0)

    ev_z = S.emit("vector", lambda e: e.memset(GACCS[:], 0.0), inc="ev")
    ev_z = S.emit("vector", lambda e: e.memset(GACC[:], 0.0), waits=[("ev", ev_z)], inc="ev")
    ev_eps = S.emit("vector", lambda e: e.memset(EPSC[:], LN_EPS), inc="ev")
    gjobs = [(mat, sl) for sl in range(NCH) for mat in range(3)]
    gstate = {"next": 0, "cons": {}}
    gv_done = {}
    gsrcs = (w1t, w2t, w3t)

    def emit_gamma_jobs(n):
        while n > 0 and gstate["next"] < len(gjobs):
            idx = gstate["next"]
            gstate["next"] += 1
            n -= 1
            mat, sl = gjobs[idx]
            sem = f"qg{mat + 1}" + ("a" if sl % 2 == 0 else "b")
            w = []
            if (mat, sl - 2) in gstate["cons"]:
                w.append(gstate["cons"][(mat, sl - 2)])
            slot = mat * 2 + (sl % 2)
            seg = GRB[:, slot * 2048:(slot + 1) * 2048]
            cq = S.dma("gpsimd", sem,
                       lambda e, seg=seg, mat=mat, sl=sl: e.dma_start(
                           out=seg.rearrange("p (c n) -> p c n", c=16),
                           in_=gsrcs[mat][sl]),
                       waits=w)
            col = GACCS[:, mat * NCH + sl:mat * NCH + sl + 1]
            if sl % 3 == 2:
                cc = S.emit("vector",
                            lambda e, seg=seg, col=col: e.reduce_sum(
                                col, seg, axis=mybir.AxisListType.X,
                                apply_absolute_value=True),
                            waits=[(sem, cq), ("ev", ev_z)], inc="ev")
                gstate["cons"][(mat, sl)] = ("ev", cc)
            else:
                cc = S.emit("scalar",
                            lambda e, seg=seg, col=col: e.activation(
                                seg, seg, AF.Abs, accum_out=col),
                            waits=[(sem, cq), ("ev", ev_z)], inc="es")
                gstate["cons"][(mat, sl)] = ("es", cc)

    def finish_gamma_scan():
        emit_gamma_jobs(len(gjobs))
        p1 = S.emit("scalar", lambda e: e.copy(GACCT[:], GACCS[:, 0:1]), inc="es")
        p2 = S.emit("scalar", lambda e: e.copy(GACCT[:], GACCS[:, 1:2]),
                    waits=[("es", p1)], inc="es")
        for mat in range(3):
            last = [gstate["cons"][(mat, sl)] for sl in range(NCH)] + [("es", p2)]
            gv_done[mat] = S.emit(
                "vector",
                lambda e, mat=mat: e.reduce_sum(
                    GACC[:, mat:mat + 1], GACCS[:, mat * NCH:(mat + 1) * NCH],
                    axis=mybir.AxisListType.X),
                waits=last, inc="ev")

    ev_dv = S.emit("vector",
                   lambda e: e.tensor_scalar(DVS[0:1, 0:2], DVS[0:1, 0:2], 0.9, 1.0,
                                             ALU.max, ALU.min),
                   waits=[("qs", q_setup)], inc="ev")
    es_ln = S.emit("scalar",
                   lambda e: e.activation(DVS[0:1, 2:4], DVS[0:1, 0:2], AF.Ln),
                   waits=[("ev", ev_dv)], inc="es")
    ep_dvb = S.emit("tensor",
                    lambda e: e.matmul(P0[:, 0:2], ONESRF[:], DVS[0:1, 2:4],
                                       start=True, stop=True),
                    waits=[("es", es_ln)], inc="ep")
    es_dvc = S.emit("scalar", lambda e: e.copy(LNDV[:], P0[:, 0:2]),
                    waits=[("ep", ep_dvb)], inc="es")
    ev_lb = None
    for i, LB in enumerate((LB0, LB1)):
        ev_lb = S.emit("vector",
                       lambda e, LB=LB, i=i: e.tensor_scalar(LB[:], OFFS[:],
                                                             LNDV[:, i:i + 1], None, ALU.mult),
                       waits=[("es", es_dvc)], inc="ev")

    def setup_half(h):
        if h == 1:
            qW = d("sync", "qs", W1ROWF[:], w1row_d)
            w = [("qs", qW)]
        else:
            w = [("qs", q_setup)]
        es_wr = S.emit("scalar", lambda e: e.copy(W1ROWBF[:], W1ROWF[:]),
                       waits=w, inc="es")
        ep_w = None
        for half in range(2):
            P = (P0, P1)[half]
            for s in range(2):
                first = (half == 0 and s == 0)
                ep_w = S.emit("tensor",
                              lambda e, P=P, half=half, s=s, first=first: (
                                  e.matmul if first else (
                                      lambda o, l, r, start, stop: mm_nold(
                                          e, o, l, r, start, stop)))(
                                  P[:, s * 512:(s + 1) * 512], ONESRB[:],
                                  W1ROWBF[0:1, half * 1024 + s * 512:half * 1024 + (s + 1) * 512],
                                  start=True, stop=True),
                              waits=[("es", es_wr)], inc="ep")
        es_wb = None
        for half in range(2):
            P = (P0, P1)[half]
            es_wb = S.emit("scalar",
                           lambda e, P=P, half=half: e.copy(
                               W1B[:, half * 1024:(half + 1) * 1024], P[:]),
                           waits=[("ep", ep_w)], inc="es")
        es_exp = None
        for m in range(16):
            DDm = DDIAG if m == 0 else DOFF
            es_exp = S.emit("scalar",
                            lambda e, m=m, DDm=DDm: e.activation(
                                EXPA[:, m * 128:(m + 1) * 128], DDm[:], AF.Exp,
                                bias=LB0[:, m:m + 1], scale=LNDV[:, 0:1]),
                            waits=[("ev", ev_lb), ("es", es_wb)], inc="es")
            es_exp = S.emit("scalar",
                            lambda e, m=m, DDm=DDm: e.activation(
                                EXPC[:, m * 128:(m + 1) * 128], DDm[:], AF.Exp,
                                bias=LB1[:, m:m + 1], scale=LNDV[:, 1:2]),
                            inc="es")
        return es_exp, es_wb

    def phase_A(h, es_setup, es_wb, qg_x):
        tile_build = {}
        tile_done = {}
        evac = {}
        stats_done = {}
        gctr = [0]

        def mixer_j(j):
            g = gctr[0]
            if USE_BCAST_AP:
                n = j + 1
                in0 = EXPC[:, 0:n * 128].rearrange("p (m q) -> p m q", m=n)
                in1 = W1B[:, j * 128:(j + 1) * 128].rearrange(
                    "p (o q) -> p o q", o=1).broadcast_to([128, n, 128])
                o3 = C2ROW[:, 0:n * 128].rearrange("p (m q) -> p m q", m=n)
                S.emit("vector",
                       lambda e, o3=o3, in0=in0, in1=in1: e.tensor_tensor(o3, in0, in1,
                                                                          ALU.mult),
                       waits=[("es", es_setup)])
            else:
                for m in range(j + 1):
                    S.emit("vector",
                           lambda e, m=m, j=j: e.tensor_tensor(
                               C2ROW[:, m * 128:(m + 1) * 128],
                               EXPC[:, m * 128:(m + 1) * 128],
                               W1B[:, j * 128:(j + 1) * 128], ALU.mult),
                           waits=[("es", es_setup)])
            P = (P0, P1)[j % 2]
            for k in range(j + 1):
                m = j - k
                slot = g % 17
                w = [("es", es_setup)]
                if g - 17 in tile_done:
                    w.append(("ep", tile_done[g - 17]))
                cv = S.emit("vector",
                            lambda e, m=m, k=k, slot=slot: e.scalar_tensor_tensor(
                                MRING[:, slot * 128:(slot + 1) * 128],
                                EXPA[:, m * 128:(m + 1) * 128], W0C[:, k:k + 1],
                                C2ROW[:, m * 128:(m + 1) * 128], ALU.mult, ALU.add),
                            waits=w, inc="ev")
                if k == j:
                    cv = S.emit("vector",
                                lambda e, slot=slot: e.tensor_tensor(
                                    MRING[:, slot * 128:(slot + 1) * 128],
                                    MRING[:, slot * 128:(slot + 1) * 128], TRI[:], ALU.mult),
                                inc="ev")
                tile_build[g] = cv
                g += 1
            for k in range(j + 1):
                gk = g - (j + 1) + k
                slot = gk % 17
                w = []
                if k == 0:
                    w = [("ev", tile_build[g - 1]), ("qx", qg_x), ("es", es_wb)]
                    if (j - 2) in evac:
                        w.append(("ev", evac[j - 2]))
                S.emit("tensor",
                       lambda e, P=P, slot=slot, k=k, j=j: e.matmul(
                           P[:, 0:512], MRING[:, slot * 128:(slot + 1) * 128],
                           XB[:, k * HT:k * HT + 512], start=(k == 0), stop=(k == j)),
                       waits=w)
                cp = S.emit("tensor",
                            lambda e, P=P, slot=slot, k=k, j=j: mm_nold(
                                e, P[:, 512:1024], MRING[:, slot * 128:(slot + 1) * 128],
                                XB[:, k * HT + 512:(k + 1) * HT], (k == 0), (k == j)),
                            inc="ep")
                tile_done[gk] = cp
            gctr[0] = g

        def post_j(j):
            P = (P0, P1)[j % 2]
            cv = S.emit("vector",
                        lambda e, P=P, j=j: e.scalar_tensor_tensor(
                            H1[:, j * HT:(j + 1) * HT], P[:], BIASC[:, j:j + 1],
                            XB[:, j * HT:(j + 1) * HT], ALU.add, ALU.add),
                        waits=[("ep", tile_done[(j + 1) * (j + 2) // 2 - 1]),
                               ("qx", qg_x)], inc="ev")
            evac[j] = cv
            w = [("ev", cv)]
            if j - 1 in stats_done:
                w.append(("ep", stats_done[j - 1]))
            cs = S.emit("scalar",
                        lambda e, j=j: e.activation(SQ[:], H1[:, j * HT:(j + 1) * HT],
                                                    AF.Square),
                        waits=w, inc="es")
            cp2 = None
            for s in range(2):
                S.emit("tensor",
                       lambda e, j=j, s=s: (e.matmul(
                           P2[0:1, s * 512:(s + 1) * 512], ONESCB[:],
                           H1[:, j * HT + s * 512:j * HT + (s + 1) * 512],
                           start=(j == 0), stop=(j == 15)) if s == 0 else mm_nold(
                           e, P2[0:1, s * 512:(s + 1) * 512], ONESCB[:],
                           H1[:, j * HT + s * 512:j * HT + (s + 1) * 512],
                           (j == 0), (j == 15))),
                       waits=[("ev", cv)])
                cp2 = S.emit("tensor",
                             lambda e, j=j, s=s: mm_nold(
                                 e, P3[0:1, s * 512:(s + 1) * 512], ONESCB[:],
                                 SQ[:, s * 512:(s + 1) * 512],
                                 (j == 0), (j == 15)),
                             waits=[("es", cs)], inc="ep")
            stats_done[j] = cp2

        for j in range(16):
            mixer_j(j)
            if j >= 1:
                post_j(j - 1)
            if h == 0:
                emit_gamma_jobs(9)
        post_j(15)
        return stats_done[15], evac

    def ln_finalize(stats_ep, last_z_ev):
        c1 = S.emit("vector",
                    lambda e: e.tensor_scalar(MROWF[:], P2[0:1, :], 1.0 / D, 0.0,
                                              ALU.mult, ALU.add),
                    waits=[("ep", stats_ep), ("ev", last_z_ev)], inc="ev")
        S.emit("vector",
               lambda e: e.tensor_tensor(VROW[:], MROWF[:], MROWF[:], ALU.mult))
        c3 = S.emit("vector",
                    lambda e: e.scalar_tensor_tensor(VROW[:], P3[0:1, :], 1.0 / D,
                                                     VROW[:], ALU.mult, ALU.subtract),
                    inc="ev")
        S.emit("scalar",
               lambda e: e.activation(VROW[:], VROW[:], AF.Ln, bias=EPSC[0:1, 0:1]),
               waits=[("ev", c3), ("ev", ev_eps)])
        c5 = S.emit("scalar",
                    lambda e: e.activation(RROW[:], VROW[:], AF.Exp, scale=-0.5),
                    inc="es")
        c6 = S.emit("vector",
                    lambda e: e.scalar_tensor_tensor(BROW[:], MROWF[:], -1.0, RROW[:],
                                                     ALU.mult, ALU.mult),
                    waits=[("es", c5)], inc="ev")
        cp = None
        for s in range(2):
            S.emit("tensor",
                   lambda e, s=s: (e.matmul(P0[:, s * 512:(s + 1) * 512], ONESRB[:],
                                            RROW[0:1, s * 512:(s + 1) * 512],
                                            start=True, stop=True) if s == 0 else
                                   mm_nold(e, P0[:, s * 512:(s + 1) * 512], ONESRB[:],
                                           RROW[0:1, s * 512:(s + 1) * 512], True, True)),
                   waits=[("ev", c6)])
            cp = S.emit("tensor",
                        lambda e, s=s: mm_nold(
                            e, P1[:, s * 512:(s + 1) * 512], ONESRB[:],
                            BROW[0:1, s * 512:(s + 1) * 512], True, True),
                        inc="ep")
        S.emit("scalar", lambda e: e.copy(RB[:], P0[:]), waits=[("ep", cp)])
        c8 = S.emit("scalar", lambda e: e.copy(BB[:], P1[:]), inc="es")
        return c8

    def gamma_finalize():
        w = [("ev", gv_done[m]) for m in range(3)]
        cp = S.emit("tensor",
                    lambda e: e.matmul(P2[0:1, 0:3], ONESCF[:], GACC[:, 0:3],
                                       start=True, stop=True),
                    waits=w, inc="ep")
        c1 = S.emit("vector",
                    lambda e: e.tensor_scalar(GROW[0:1, 0:3], P2[0:1, 0:3], 1.0 / NW,
                                              G_EPS, ALU.mult, ALU.add),
                    waits=[("ep", cp)], inc="ev")
        c2 = S.emit("vector",
                    lambda e: e.reciprocal(GROW[0:1, 4:7], GROW[0:1, 0:3]),
                    waits=[("ev", c1)], inc="ev")
        cp2 = S.emit("tensor",
                     lambda e: e.matmul(P2[:, 0:3], ONESRF[:], GROW[0:1, 4:7],
                                        start=True, stop=True),
                     waits=[("ev", c2)], inc="ep")
        S.emit("scalar", lambda e: e.copy(RG[:], P2[:, 0:3]),
               waits=[("ep", cp2)])
        c3 = S.emit("scalar",
                    lambda e: e.activation(RGN[:], RG[:], AF.Copy, scale=-1.0),
                    inc="es")
        return c3

    def normalize_bf16(ln_es):
        last = None
        for j in range(16):
            S.emit("vector",
                   lambda e, j=j: e.tensor_tensor(
                       H1[:, j * HT:(j + 1) * HT], H1[:, j * HT:(j + 1) * HT],
                       RB[:], ALU.mult),
                   waits=[("es", ln_es)])
            last = S.emit("vector",
                          lambda e, j=j: e.tensor_tensor(
                              H1[:, j * HT:(j + 1) * HT], H1[:, j * HT:(j + 1) * HT],
                              BB[:], ALU.add),
                          inc="ev")
        return last

    def normalize_store(ln_es, h):
        stage_dma = {}
        for j in range(16):
            S.emit("vector",
                   lambda e, j=j: e.tensor_tensor(
                       TNORM[:], H1[:, j * HT:(j + 1) * HT], RB[:], ALU.mult),
                   waits=[("es", ln_es)])
            for s in range(2):
                u = j * 2 + s
                STG = (STAGEA, STAGEB)[u % 2]
                par = "qST" + ("a" if u % 2 == 0 else "b")
                w = []
                if u - 2 in stage_dma:
                    w.append(stage_dma[u - 2])
                cv = S.emit("vector",
                            lambda e, s=s, STG=STG: e.tensor_tensor(
                                STG[:], TNORM[:, s * 512:(s + 1) * 512],
                                BB[:, s * 512:(s + 1) * 512], ALU.add),
                            waits=w, inc="ev")
                stage_dma[u] = (par, S.dma(
                    "sync", par,
                    lambda e, j=j, s=s, STG=STG: e.dma_start(
                        out=out_fm[j * 128:(j + 1) * 128,
                                   h * HT + s * 512:h * HT + (s + 1) * 512],
                        in_=STG[:]),
                    waits=[("ev", cv)]))

    def phase_B(h, rg_es, norm_ev):
        mult_done = {}
        sig_done = {}
        pe_done = {}
        gp_done = {}
        q_ready_d = {}
        QMAT = {0: (L1, Q1R, w1t, q1c, "sync", "qL1", "qc1"),
                1: (L2, Q2R, w2t, q2c, "scalar", "qL2", "qc2"),
                2: (L3, Q3S, w3t, q3c, "gpsimd", "qL3", "qc3")}

        def quant(mat, hb):
            L, QOUT, src_t, qcache, eng_q, semL, semC = QMAT[mat]
            hs = hb % 2
            for piece in range(2):
                u = 2 * hb + piece
                slot = u % 2
                sem = semL + ("a" if slot == 0 else "b")
                seg = L[:, slot * 1024:(slot + 1) * 1024]
                w = []
                if (mat, u - 2) in gp_done:
                    w.append(("ev", gp_done[(mat, u - 2)]))
                cq = S.dma(eng_q, sem,
                           lambda e, seg=seg, src=src_t[hb], piece=piece: e.dma_start(
                               out=seg.rearrange("p (c n) -> p c n", c=8),
                               in_=src[:, piece * 8:(piece + 1) * 8, :]),
                           waits=w)
                qcol = hs * D + piece * 1024
                w = [(sem, cq), ("es", rg_es), (semC, S.cnt[semC])]
                if hb - 2 in pe_done and mat < 2:
                    w.append(("ep", pe_done[hb - 2]))
                S.emit("vector",
                       lambda e, seg=seg, QOUT=QOUT, qcol=qcol, mat=mat: e.tensor_scalar(
                           QOUT[:, qcol:qcol + 1024], seg, RG[:, mat:mat + 1], 0.5,
                           ALU.mult, ALU.is_ge),
                       waits=w)
                S.emit("vector",
                       lambda e, seg=seg, mat=mat: e.tensor_scalar(
                           SQ[:], seg, RGN[:, mat:mat + 1], 0.5,
                           ALU.mult, ALU.is_ge))
                cg = S.emit("vector",
                            lambda e, QOUT=QOUT, qcol=qcol: e.tensor_tensor(
                                QOUT[:, qcol:qcol + 1024], QOUT[:, qcol:qcol + 1024],
                                SQ[:], ALU.subtract),
                            inc="ev")
                gp_done[(mat, u)] = cg
            co = (q3c[hb].rearrange("p c n -> p (c n)") if mat == 2 else qcache[hb])
            S.dma(eng_q, semC,
                  lambda e, co=co, QOUT=QOUT, hs=hs: e.dma_start(
                      out=co, in_=QOUT[:, hs * D:(hs + 1) * D]),
                  waits=[("ev", gp_done[(mat, 2 * hb + 1)])])

        def produce(hb):
            hs = hb % 2
            if h == 0:
                for mat in range(3):
                    quant(mat, hb)
            else:
                w = []
                if hb - 2 in pe_done:
                    w.append(("ep", pe_done[hb - 2]))
                par = "a" if hs == 0 else "b"
                c1 = S.dma("sync", "qR1" + par,
                           lambda e, hs=hs, hb=hb: e.dma_start(
                               out=Q1R[:, hs * D:(hs + 1) * D], in_=q1c[hb]), w)
                c2 = S.dma("scalar", "qR2" + par,
                           lambda e, hs=hs, hb=hb: e.dma_start(
                               out=Q2R[:, hs * D:(hs + 1) * D], in_=q2c[hb]), w)
                q_ready_d[hb] = [("qR1" + par, c1), ("qR2" + par, c2)]

        def consume(hb):
            hs = hb % 2
            if h == 0:
                q_ready = [("ev", gp_done[(0, 2 * hb + 1)]),
                           ("ev", gp_done[(1, 2 * hb + 1)])]
            else:
                q_ready = q_ready_d[hb]
            PA, PB = (P0, P1) if hs == 0 else (P2, P3)
            w = list(q_ready) + [("ev", norm_ev)]
            if hb - 2 in mult_done:
                w.append(("ev", mult_done[hb - 2]))
            cp = None
            for QR, PX in ((Q1R, PA), (Q2R, PB)):
                for dc in range(NCD):
                    for s in range(2):
                        last = (QR is Q2R and dc == NCD - 1 and s == 1)
                        cp = S.emit("tensor",
                                    lambda e, QR=QR, PX=PX, hs=hs, dc=dc, s=s: (e.matmul(
                                        PX[:, s * 512:(s + 1) * 512],
                                        QR[:, hs * D + dc * 128:hs * D + (dc + 1) * 128],
                                        H1[:, dc * HT + s * 512:dc * HT + (s + 1) * 512],
                                        start=(dc == 0), stop=(dc == NCD - 1)) if s == 0
                                        else mm_nold(
                                        e, PX[:, s * 512:(s + 1) * 512],
                                        QR[:, hs * D + dc * 128:hs * D + (dc + 1) * 128],
                                        H1[:, dc * HT + s * 512:dc * HT + (s + 1) * 512],
                                        (dc == 0), (dc == NCD - 1))),
                                    waits=(w if QR is Q1R and dc == 0 and s == 0 else ()),
                                    inc=("ep" if last else None))
            pe_done[hb] = cp
            w = [("ep", cp)]
            if hb - 1 in mult_done:
                w.append(("ev", mult_done[hb - 1]))
            cs = S.emit("scalar",
                        lambda e, PA=PA: e.activation(SIG[:], PA[:], AF.Sigmoid),
                        waits=w, inc="es")
            sig_done[hb] = cs
            cv = S.emit("vector",
                        lambda e, PB=PB, hb=hb: e.tensor_tensor(
                            HID[:, hb * HT:(hb + 1) * HT], PB[:], SIG[:], ALU.mult),
                        waits=[("es", cs)], inc="ev")
            mult_done[hb] = cv

        for hb in range(NCH):
            produce(hb)
            if hb >= 1:
                consume(hb - 1)
        consume(NCH - 1)
        return mult_done[NCH - 1]

    def phase_C(h, hid_ev):
        evac = {}
        stats_done = {}
        pe_f = {}

        col_cnt = {}

        def col_dma(dco):
            slot = dco % 2
            w = []
            if dco - 2 in pe_f:
                w.append(("ep", pe_f[dco - 2]))
            col_cnt[dco] = S.dma(
                "sync", "qCC" + ("a" if slot == 0 else "b"),
                lambda e, slot=slot, dco=dco: e.dma_start(
                    out=QC[:, slot * 5632:(slot + 1) * 5632].rearrange(
                        "p (a b) -> p a b", a=NCH),
                    in_=q3c[:, :, dco, :].rearrange("a p b -> p a b")),
                waits=w)

        def cpe(dco):
            slot = dco % 2
            P = (P0, P1)[slot]
            w = [("qCC" + ("a" if slot == 0 else "b"), col_cnt[dco]), ("ev", hid_ev)]
            if dco - 2 in evac:
                w.append(("ev", evac[dco - 2]))
            cp = None
            for k in range(NCH):
                for s in range(2):
                    last = (k == NCH - 1 and s == 1)
                    cp = S.emit("tensor",
                                lambda e, P=P, slot=slot, k=k, s=s: (e.matmul(
                                    P[:, s * 512:(s + 1) * 512],
                                    QC[:, slot * 5632 + k * 128:slot * 5632 + (k + 1) * 128],
                                    HID[:, k * HT + s * 512:k * HT + (s + 1) * 512],
                                    start=(k == 0), stop=(k == NCH - 1)) if s == 0
                                    else mm_nold(
                                    e, P[:, s * 512:(s + 1) * 512],
                                    QC[:, slot * 5632 + k * 128:slot * 5632 + (k + 1) * 128],
                                    HID[:, k * HT + s * 512:k * HT + (s + 1) * 512],
                                    (k == 0), (k == NCH - 1))),
                                waits=(w if k == 0 and s == 0 else ()),
                                inc=("ep" if last else None))
            pe_f[dco] = cp

        def cpost(dco):
            P = (P0, P1)[dco % 2]
            cv = S.emit("vector",
                        lambda e, P=P, dco=dco: e.tensor_tensor(
                            H1[:, dco * HT:(dco + 1) * HT], P[:],
                            H1[:, dco * HT:(dco + 1) * HT], ALU.add),
                        waits=[("ep", pe_f[dco])], inc="ev")
            evac[dco] = cv
            w = [("ev", cv)]
            if dco - 1 in stats_done:
                w.append(("ep", stats_done[dco - 1]))
            cs = S.emit("scalar",
                        lambda e, dco=dco: e.activation(SQ[:], H1[:, dco * HT:(dco + 1) * HT],
                                                        AF.Square),
                        waits=w, inc="es")
            cp2 = None
            for s in range(2):
                S.emit("tensor",
                       lambda e, dco=dco, s=s: (e.matmul(
                           P2[0:1, s * 512:(s + 1) * 512], ONESCB[:],
                           H1[:, dco * HT + s * 512:dco * HT + (s + 1) * 512],
                           start=(dco == 0), stop=(dco == NCD - 1)) if s == 0 else
                           mm_nold(
                           e, P2[0:1, s * 512:(s + 1) * 512], ONESCB[:],
                           H1[:, dco * HT + s * 512:dco * HT + (s + 1) * 512],
                           (dco == 0), (dco == NCD - 1))),
                       waits=[("ev", cv)])
                cp2 = S.emit("tensor",
                             lambda e, dco=dco, s=s: mm_nold(
                                 e, P3[0:1, s * 512:(s + 1) * 512], ONESCB[:],
                                 SQ[:, s * 512:(s + 1) * 512],
                                 (dco == 0), (dco == NCD - 1)),
                             waits=[("es", cs)], inc="ep")
            stats_done[dco] = cp2

        for dco in range(NCD):
            col_dma(dco)
            if dco >= 1:
                cpe(dco - 1)
            if dco >= 2:
                cpost(dco - 2)
        cpe(NCD - 1)
        cpost(NCD - 2)
        cpost(NCD - 1)
        ln_es = ln_finalize(stats_done[NCD - 1], evac[NCD - 1])
        normalize_store(ln_es, h)

    es_setup, es_wb = setup_half(0)
    stats_ep, evacA = phase_A(0, es_setup, es_wb, qg_x0)
    ln_es = ln_finalize(stats_ep, evacA[15])
    norm_ev = normalize_bf16(ln_es)
    if dump_h1:
        d("sync", "qs", dbg_h1[0], H1[:], waits=[("ev", norm_ev)])
    if phases >= 2:
        finish_gamma_scan()
        rg_es = gamma_finalize()
        S.barrier()
        hid_ev = phase_B(0, rg_es, norm_ev)
        if dump_hid:
            d("sync", "qs", dbg_hid[0], HID[:], waits=[("ev", hid_ev)])
        if dump_q:
            S.barrier()
            d("sync", "qs", dbg_q1[:], q1c)
            d("sync", "qs", dbg_q2[:], q2c)
            d("sync", "qs", dbg_rg, RG[:])
            d("sync", "qs", dbg_ga, GACC[:])
    if phases >= 3:
        S.barrier()
        phase_C(0, hid_ev)
        S.barrier()
        qg_x1 = x_dma(1)
        es_setup2, es_wb2 = setup_half(1)
        stats_ep2, evacA2 = phase_A(1, es_setup2, es_wb2, qg_x1)
        ln_es2 = ln_finalize(stats_ep2, evacA2[15])
        norm_ev2 = normalize_bf16(ln_es2)
        if dump_h1:
            d("sync", "qs", dbg_h1[1], H1[:], waits=[("ev", norm_ev2)])
        S.barrier()
        hid_ev2 = phase_B(1, rg_es, norm_ev2)
        if dump_hid:
            d("sync", "qs", dbg_hid[1], HID[:], waits=[("ev", hid_ev2)])
        S.barrier()
        phase_C(1, hid_ev2)

    qfin = [(n, S.cnt[n]) for n in S.cnt if n.startswith("q")]
    S.emit("sync", lambda e: e.nop(), waits=qfin)

    with contextlib.ExitStack() as stack:
        sems = {n: stack.enter_context(nc.semaphore(n)) for n in S.cnt}
        block = stack.enter_context(nc.Block())

        def make_body(eng_name):
            def body(e):
                for waits, fn, inc, amt in S.ops[eng_name]:
                    for s, v in waits:
                        e.wait_ge(sems[s], v)
                    ins = fn(e)
                    if inc is not None:
                        ins.then_inc(sems[inc], amt)
            return body

        for eng_name in ENGS:
            getattr(block, eng_name)(make_body(eng_name))
    return nc


def _host_layout(L0_attn_x, L0_attn_weight, L0_attn_bias, L0_attn_decay_value,
                 L0_ffn_w1, L0_ffn_w2, L0_ffn_w3):
    f32 = np.float32
    x2 = np.ascontiguousarray(np.asarray(L0_attn_x, f32).reshape(8 * TOK, D))
    w1t = np.ascontiguousarray(np.asarray(L0_ffn_w1, f32).reshape(NCH, 128, NCD, 128)
                               .transpose(0, 3, 2, 1))
    w2t = np.ascontiguousarray(np.asarray(L0_ffn_w2, f32).reshape(NCH, 128, NCD, 128)
                               .transpose(0, 3, 2, 1))
    w3t = np.ascontiguousarray(np.asarray(L0_ffn_w3, f32).reshape(NCD, 128, NCH, 128)
                               .transpose(2, 3, 0, 1))
    wt = np.asarray(L0_attn_weight, f32)
    w0c = np.ascontiguousarray(wt[0].reshape(NCD, 128).T)
    w1row = np.ascontiguousarray(wt[1].reshape(1, D))
    biasc = np.ascontiguousarray(np.asarray(L0_attn_bias, f32).reshape(NCD, 128).T)
    dv = np.ascontiguousarray(np.asarray(L0_attn_decay_value, f32).reshape(1, 2))

    ii = np.arange(128, dtype=f32)
    dmat = ii[None, :] - ii[:, None]
    shared = {
        "c_ddiag": np.ascontiguousarray(np.maximum(dmat, 0.0).astype(f32)),
        "c_doff": np.ascontiguousarray(dmat.astype(f32)),
        "c_tri": np.ascontiguousarray((dmat >= 0).astype(ml_dtypes.bfloat16)),
        "c_offs": np.ascontiguousarray(
            np.broadcast_to(128.0 * np.arange(16, dtype=f32), (128, 16))),
        "c_ones_r_bf": np.ones((1, 128), ml_dtypes.bfloat16),
        "c_ones_c_bf": np.ones((128, 1), ml_dtypes.bfloat16),
        "c_ones_r_f": np.ones((1, 128), f32),
        "c_ones_c_f": np.ones((128, 1), f32),
        "w1t": w1t, "w2t": w2t, "w3t": w3t, "w0c": w0c, "biasc": biasc,
        "dv": dv, "w1row": w1row,
    }
    in_maps = []
    for c in range(8):
        m = dict(shared)
        m["x_fm"] = np.ascontiguousarray(x2[c * TOK:(c + 1) * TOK, :].T)
        in_maps.append(m)
    return in_maps


def kernel(L0_attn_x, L0_attn_weight, L0_attn_bias, L0_attn_decay_value,
           L0_ffn_w1, L0_ffn_w2, L0_ffn_w3):
    in_maps = _host_layout(L0_attn_x, L0_attn_weight, L0_attn_bias,
                           L0_attn_decay_value, L0_ffn_w1, L0_ffn_w2, L0_ffn_w3)
    nc = build_program()
    res = run_bass_kernel_spmd(nc, in_maps, core_ids=list(range(8)))
    parts = [np.asarray(r["out_fm"]).T for r in res.results]
    return np.ascontiguousarray(np.concatenate(parts, axis=0)
                                .reshape(2, 8192, D)).astype(np.float32)
